# version 02
# baseline (speedup 1.0000x reference)
import numpy as np
import ml_dtypes
import concourse.bass as bass
import concourse.mybir as mybir
from concourse.bass_utils import run_bass_kernel_spmd
from contextlib import ExitStack

F32 = mybir.dt.float32
BF16 = mybir.dt.bfloat16
U16 = mybir.dt.uint16
I32 = mybir.dt.int32
AF = mybir.ActivationFunctionType
ALU = mybir.AluOpType
AX = mybir.AxisListType

TQ, TK, D = 1024, 2048, 2048
NH, KH, HD = 16, 8, 128
DC, NTT, NSC = 16, 8, 16
MLP = 16384
GSETS = 8
NGRP = 16
ATTN_SCALE = 0.08838834764831845
TWO_PI = float(2 * np.pi)
HALF_PI = float(np.pi / 2)

TAPS = ()
STAGE = 99
DOWN_GRPS = None
DBG_CB = None

K1 = 1024


def build_program(taps=(), stage=99):
    nc = bass.Bass("TRN2", target_bir_lowering=False, debug=(DBG_CB is not None))

    x_kv = nc.dram_tensor("x_kv", [TK, D], F32, kind="ExternalInput").ap()
    val_in = nc.dram_tensor("val_in", [TK], F32, kind="ExternalInput").ap()
    koff_in = nc.dram_tensor("koff_in", [128], F32, kind="ExternalInput").ap()
    invts_in = nc.dram_tensor("invts_in", [128, 64], F32, kind="ExternalInput").ap()
    qkw = nc.dram_tensor("qkw", [NH, D, HD], F32, kind="ExternalInput").ap()
    kvk = nc.dram_tensor("kvk", [2, KH, D, HD], F32, kind="ExternalInput").ap()
    okw = nc.dram_tensor("okw", [NH, HD, D], F32, kind="ExternalInput").ap()
    qns = nc.dram_tensor("qns", [1, HD], F32, kind="ExternalInput").ap()
    kns = nc.dram_tensor("kns", [1, HD], F32, kind="ExternalInput").ap()
    g_pk = nc.dram_tensor("g_pk", [MLP, D // 2], U16, kind="ExternalInput").ap()
    g_sc = nc.dram_tensor("g_sc", [MLP, 64], F32, kind="ExternalInput").ap()
    u_pk = nc.dram_tensor("u_pk", [MLP, D // 2], U16, kind="ExternalInput").ap()
    u_sc = nc.dram_tensor("u_sc", [MLP, 64], F32, kind="ExternalInput").ap()
    d_pk = nc.dram_tensor("d_pk", [D, MLP // 2], U16, kind="ExternalInput").ap()
    d_sc = nc.dram_tensor("d_sc", [D, MLP // 32], F32, kind="ExternalInput").ap()
    out = nc.dram_tensor("out", [TQ, D], F32, kind="ExternalOutput").ap()

    tap_aps = {}

    def tap(name, shape, dtype):
        if name in taps:
            tap_aps[name] = nc.dram_tensor("tap_" + name, shape, dtype,
                                           kind="ExternalOutput").ap()
            return tap_aps[name]
        return None

    t_xt = tap("xt", [128, DC * TK], BF16)
    t_rop0 = tap("rop0", [128, 2048], F32)
    t_ktb = tap("ktb", [128, 4096], F32)
    t_dbg = tap("dbg", [128, 1024], F32)
    t_kt = tap("kt", [128, KH * TK], BF16)
    t_v = tap("v", [128, NSC * KH * HD], BF16)
    t_qt = tap("qt", [128, NH * TQ], BF16)
    t_enc = tap("enc", [128, NH * TQ], BF16)
    t_y = tap("y", [TQ, D], F32)
    t_int = tap("int", [128, GSETS * TQ], BF16)
    t_e2 = tap("e2", [TQ, D], F32)

    es = ExitStack()
    with es:
        def sb(name, shape, dt):
            return es.enter_context(nc.sbuf_tensor(name, shape, dt))

        def psmt(name, shape):
            return es.enter_context(nc.psum_tensor(name, shape, F32))

        scr = sb("scr", [128, 1024], F32)
        qtab = sb("qtab", [128, 8, 4, 64], F32)
        stat = sb("stat", [128, 16], F32)
        prjw = sb("prj", [128, 1024], F32)
        prj = prjw[:, 0:512]
        rop = sb("rop", [128, 512], BF16)
        rbc = sb("rbc", [128, 512], F32)
        valv = sb("valv", [128, NSC], F32)
        koff = sb("koff", [128, 1], F32)
        invts = sb("invts", [128, 64], F32)
        piota = sb("piota", [128, 1], I32)
        piof = sb("piof", [128, 1], F32)
        onesb = sb("onesb", [128, 1], BF16)
        onesf1 = sb("onesf1", [1, 128], F32)
        epsln = sb("epsln", [128, 1], F32)
        epsrm = sb("epsrm", [128, 1], F32)
        qksc = sb("qksc", [1, 256], F32)

        AR = sb("AR", [128, 92 * K1], U16)

        def AV(o_k1, n_u16, dt, shape=None):
            ap = AR[:, o_k1 * K1:o_k1 * K1 + n_u16].bitcast(dt)
            if shape is not None:
                lead = "p (" + " ".join(f"a{i}" for i in range(len(shape))) + ") -> p " + \
                   " ".join(f"a{i}" for i in range(len(shape)))
                kw = {f"a{i}": s for i, s in enumerate(shape)}
                ap = ap.rearrange(lead, **kw)
            return ap

        kT = AV(0, 16 * K1, BF16, (KH, TK))
        masks = AV(32, 4 * K1, BF16, (8, 512))
        v_arr = AV(16, 16 * K1, BF16, (NSC, KH, HD))
        e_t = [AV(4 * tt, 4 * K1, F32, (D,)) for tt in range(8)]
        xT_q = AV(32, 16 * K1, BF16, (DC, TQ))
        wo_grp = [AV(32 + 8 * i, 8 * K1, BF16, (NH, 512)) for i in range(2)]
        planes = [AV(32 + 4 * i, 4 * K1, BF16, (4, 1024)) for i in range(2)]
        WT = [AV(40 + 4 * i, 4 * K1, BF16, (32, 128)) for i in range(2)]
        xT_k = AV(48, 16 * K1, BF16, (DC, TQ))
        qT = AV(48, 16 * K1, BF16, (NH, TQ))
        inter = AV(48, 8 * K1, BF16, (GSETS, TQ))
        WdT = [AV(56, 4 * K1, BF16, (8, 512)), AV(60, 4 * K1, BF16, (8, 512))]
        ropet = AV(64, 4 * K1, F32, (32, 64))
        wgrp = [AV(64 + 8 * i, 8 * K1, BF16, (DC, 512)) for i in range(2)]
        enc = AV(64, 16 * K1, BF16, (NH, TQ))
        yTe = AV(64, 8 * K1, BF16, (8, TQ))
        yTo = AV(72, 8 * K1, BF16, (8, TQ))
        xf32 = [AV(80 + 4 * i, 4 * K1, F32, (D,)) for i in range(2)]
        ktab = AV(80, 8 * K1, F32, (16, 4, 64))
        ptil = AV(80, 6 * K1, BF16, (12, 512))
        lnscr = AV(80, 4 * K1, F32, (D,))
        gupk = [AV(80 + 2 * i, 2 * K1, U16, (2, 1024)) for i in range(4)]
        xbf = [AV(88 + 2 * i, 2 * K1, BF16, (D,)) for i in range(2)]
        yev = AV(88, 2 * K1, BF16, (D,))
        wstg = AV(88, 2 * K1, F32, (1024,))
        dnp2 = qtab[:].rearrange("p a b c -> p (a b c)")[:, 1536:2048].bitcast(
            U16).rearrange("p (a b) -> p a b", a=2)
        dnpk = [AV(88 + i, 1 * K1, U16, (2, 512)) for i in range(2)] + [dnp2]
        dplanes = AV(90, 2 * K1, BF16, (2, 1024))

        dscT = scr[:].bitcast(U16)
        dscS = prjw[:].bitcast(U16)
        plv_bf = [AV(32 + 4 * i, 4 * K1, BF16, (2, 2, 1024)) for i in range(2)]
        plv_u16 = [AV(32 + 4 * i, 4 * K1, U16, (2, 2, 1024)) for i in range(2)]
        dpl_bf = AV(90, 2 * K1, BF16, (2, 2, 512))
        dpl_u16 = AV(90, 2 * K1, U16, (2, 2, 512))
        glscr = rop[:]
        gscf = sb("gscf", [128, 8, 64], F32)
        dscf = sb("dscf", [128, 6, 32], F32)

        pp = [psmt(f"pp{i}", [128, 512]) for i in range(6)]
        pp_r = psmt("pp_r", [128, 512])
        pp6 = psmt("pp6", [128, 512])

        SEM, C = {}, {}
        for s in ("dC", "dX0", "dX1", "dW0", "dW1", "dPK0", "dPK1", "dDN0",
                  "dDN1", "dT0", "dT1", "dU0", "dU1", "dXB", "dST", "v", "a",
                  "t", "p"):
            SEM[s] = es.enter_context(nc.semaphore(s))
            C[s] = 0

        def inc(instr, s, k=1):
            instr.then_inc(SEM[s], k)
            C[s] += k

        def wt(eng, s, v=None):
            eng.wait_ge(SEM[s], C[s] if v is None else v)

        DSEMS = ("dC", "dX0", "dX1", "dW0", "dW1", "dPK0", "dPK1", "dDN0",
                 "dDN1", "dT0", "dT1", "dU0", "dU1", "dXB", "dST")
        _waited = {sname: 0 for sname in DSEMS}
        _rstate = {"xp": False}

        def ring_sync(sy):
            for sname in DSEMS:
                if C[sname] > _waited[sname]:
                    sy.wait_ge(SEM[sname], C[sname])
                    _waited[sname] = C[sname]

        def VC(v_, instr):
            inc(instr, "v")
            wt(v_, "v")

        def ring_dma(sy, *a, transpose=False, **kw):
            if transpose or _rstate["xp"]:
                ring_sync(sy)
            _rstate["xp"] = transpose
            return sy.dma_start(*a, transpose=transpose, **kw)

        block = es.enter_context(nc.Block())

        @block.sync
        def _(sy):
            for d1 in (
                ring_dma(sy, out=koff[:], in_=koff_in.rearrange("(p o) -> p o", o=1)),
                ring_dma(sy, out=invts[:], in_=invts_in),
                ring_dma(sy, out=valv[:], in_=val_in.rearrange("(c p) -> p c", p=128),
                             allow_slow_non_contiguous=True),
                ring_dma(sy, out=qksc[:, 0:128], in_=qns),
                ring_dma(sy, out=qksc[:, 128:256], in_=kns),
            ):
                inc(d1, "dC", 16)

        @block.gpsimd
        def _(g_):
            g_.iota(piota[:], pattern=[[0, 1]], base=0, channel_multiplier=1)
            inc(g_.memset(onesb[:], 1.0), "p")

        @block.vector
        def _(v_):
            wt(v_, "dC")
            wt(v_, "p")
            VC(v_, v_.tensor_copy(piof[:], piota[:]))
            v_.memset(onesf1[:], 1.0)
            v_.memset(epsln[:], 1e-5)
            VC(v_, v_.memset(epsrm[:], 1e-6))
            arg = scr[:, 0:64]
            n1 = scr[:, 64:128]
            ni = scr[:, 128:192].bitcast(I32)[:, 0:64]
            posf = scr[:, 192:193]
            for c in range(16):
                VC(v_, v_.tensor_scalar(posf, piota[:], koff[:, 0:1], float(128 * c),
                                        ALU.add, ALU.add))
                VC(v_, v_.tensor_scalar(arg, invts[:], posf, None, ALU.mult))
                for j, shift in ((0, 0.0), (1, HALF_PI)):
                    dst = ropet[:, 2 * c + j, :]
                    VC(v_, v_.tensor_scalar(n1, arg, shift, 1.0 / TWO_PI, ALU.add, ALU.mult))
                    VC(v_, v_.tensor_copy(ni, n1))
                    VC(v_, v_.tensor_scalar(n1, arg, shift, None, ALU.add))
                    VC(v_, v_.scalar_tensor_tensor(dst, ni, -TWO_PI, n1, ALU.mult, ALU.add))
                    VC(v_, v_.tensor_scalar(n1, dst, float(np.pi), None, ALU.is_gt))
                    VC(v_, v_.scalar_tensor_tensor(dst, n1, -TWO_PI, dst, ALU.mult, ALU.add))
                    VC(v_, v_.tensor_scalar(n1, dst, float(-np.pi), None, ALU.is_lt))
                    VC(v_, v_.scalar_tensor_tensor(dst, n1, TWO_PI, dst, ALU.mult, ALU.add))
            iv = v_.tensor_copy(scr[:, 256:257], piof[:])
            inc(iv, "v")
        v_args = C["v"]

        @block.scalar
        def _(s_):
            wt(s_, "v", v_args)
            a1 = s_.activation(ropet[:].rearrange("p a b -> p (a b)"),
                               ropet[:].rearrange("p a b -> p (a b)"), AF.Sin)
            inc(a1, "a")
        a_sin = C["a"]

        @block.tensor
        def _(t_):
            wt(t_, "a", a_sin)
            t_.matmul(pp[0][:, 0:128], onesf1[:], qksc[0:1, 0:128], start=True, stop=True)
            tm = t_.matmul(pp[0][:, 128:256], onesf1[:], qksc[0:1, 128:256],
                           start=True, stop=True)
            inc(tm, "t")
        t_scb = C["t"]

        @block.vector
        def _(v_):
            wt(v_, "t", t_scb)
            qsR = scr[:, 0:128]
            ksR = scr[:, 128:256]
            VC(v_, v_.tensor_copy(qsR, pp[0][:, 0:128]))
            VC(v_, v_.tensor_copy(ksR, pp[0][:, 128:256]))
            for c in range(8, 16):
                sinc, cosc = ropet[:, 2 * c, :], ropet[:, 2 * c + 1, :]
                v_.tensor_tensor(qtab[:, c - 8, 0, :], cosc, qsR[:, 0:64], ALU.mult)
                v_.tensor_tensor(qtab[:, c - 8, 1, :], cosc, qsR[:, 64:128], ALU.mult)
                v_.tensor_tensor(qtab[:, c - 8, 2, :], sinc, qsR[:, 0:64], ALU.mult)
                kv4 = v_.tensor_tensor(qtab[:, c - 8, 3, :], sinc, qsR[:, 64:128],
                                       ALU.mult)
            VC(v_, kv4)
            iv2 = v_.tensor_copy(scr[:, 256:257], piof[:])
            inc(iv2, "v")
        v_tabs = C["v"]

        for sc_i in range(NSC):
            sl = sc_i % 2

            @block.sync
            def _(sy, sc_i=sc_i, sl=sl):
                if sc_i >= 2:
                    wt(sy, "v", C[f"_xc{sc_i - 2}"])
                d1 = ring_dma(sy, out=xf32[sl][:],
                                  in_=x_kv[128 * sc_i:128 * (sc_i + 1), :])
                inc(d1, "dX%d" % sl, 16)

            @block.vector
            def _(v_, sc_i=sc_i, sl=sl):
                wt(v_, "dX%d" % sl, 16 * (sc_i // 2 + 1))
                if sc_i == 0:
                    wt(v_, "v", v_tabs)
                if sc_i >= 2:
                    wt(v_, "dXB", 16 * (sc_i - 1))
                cv = v_.tensor_copy(xbf[sl][:], xf32[sl][:])
                inc(cv, "v")
                C[f"_xc{sc_i}"] = C["v"]

            @block.sync
            def _(sy, sc_i=sc_i, sl=sl):
                wt(sy, "v", C[f"_xc{sc_i}"])
                dst = xT_k if sc_i < 8 else xT_q
                so = sc_i if sc_i < 8 else sc_i - 8
                d1 = ring_dma(sy, out=dst[:, :, 128 * so:128 * (so + 1)],
                                  in_=xbf[sl][:], transpose=True)
                inc(d1, "dXB", 16)
        dXB_xt = C["dXB"]
        C["_p1_v"] = C["v"]

        @block.vector
        def _(v_):
            wt(v_, "dXB", dXB_xt)
            ksR = scr[:, 128:256]
            for c in range(16):
                sinc, cosc = ropet[:, 2 * c, :], ropet[:, 2 * c + 1, :]
                v_.tensor_tensor(ktab[:, c, 0, :], cosc, ksR[:, 0:64], ALU.mult)
                v_.tensor_tensor(ktab[:, c, 1, :], cosc, ksR[:, 64:128], ALU.mult)
                v_.tensor_tensor(ktab[:, c, 2, :], sinc, ksR[:, 0:64], ALU.mult)
                kv4 = v_.tensor_tensor(ktab[:, c, 3, :], sinc, ksR[:, 64:128], ALU.mult)
            VC(v_, kv4)
            iv3 = v_.tensor_copy(scr[:, 256:257], piof[:])
            inc(iv3, "v")
        v_ktab = C["v"]

        if t_rop0 is not None:
            @block.sync
            def _(sy):
                wt(sy, "v", v_ktab)
                inc(ring_dma(sy, out=t_rop0, in_=ropet[:].rearrange("p a b -> p (a b)")), "dST", 16)
                inc(ring_dma(sy, out=t_ktb, in_=ktab[:].rearrange("p a b c -> p (a b c)")), "dST", 16)

        if t_xt is not None:
            @block.sync
            def _(sy):
                wt(sy, "dXB", dXB_xt)
                d0 = ring_dma(sy, out=t_xt.rearrange("p (d t) -> p d t", d=DC)[:, :, 0:1024],
                                  in_=xT_k[:])
                inc(d0, "dST", 16)
                d1 = ring_dma(sy, 
                    out=t_xt.rearrange("p (d t) -> p d t", d=DC)[:, :, 1024:2048],
                    in_=xT_q[:])
                inc(d1, "dST", 16)

        def load_wgrp(dst, col_aps, vkey, extra_ring_wait=None, cast_wait=None):
            nb = len(col_aps)
            for i_ in range(nb):
                wsl = i_ % 2

                @block.sync
                def _(sy, i_=i_, wsl=wsl):
                    if extra_ring_wait is not None and i_ == 0:
                        for sem_name, valw in extra_ring_wait:
                            sy.wait_ge(SEM[sem_name], valw)
                    if i_ >= 2:
                        wt(sy, "v", C[f"_wc_{vkey}_{i_ - 2}"])
                    dst_ap = wstg[:, 512 * wsl:512 * (wsl + 1)]
                    if col_aps[i_].ndim == 3:
                        dst_ap = dst_ap.rearrange("p (a b) -> p a b",
                                                  b=col_aps[i_].shape[2])
                    d1 = ring_dma(sy, out=dst_ap, in_=col_aps[i_])
                    inc(d1, "dW%d" % wsl, 16)

                @block.vector
                def _(v_, i_=i_, wsl=wsl):
                    wt(v_, "dW%d" % wsl, C["dW%d" % wsl])
                    if cast_wait is not None and i_ == 0:
                        v_.wait_ge(SEM[cast_wait[0]], cast_wait[1])
                    cv = v_.tensor_copy(dst(i_), wstg[:, 512 * wsl:512 * (wsl + 1)])
                    inc(cv, "v")
                    C[f"_wc_{vkey}_{i_}"] = C["v"]
            C[f"_wg_{vkey}"] = C["v"]

        def kv_cols(c, gidx):
            cc, gg = gidx // 2, gidx % 2
            return kvk[cc, :, :, :].rearrange("k d h -> d k h")[
                128 * c:128 * (c + 1), 4 * gg:4 * (gg + 1), :]

        def q_cols(c, g):
            return qkw.rearrange("n d h -> d n h")[128 * c:128 * (c + 1),
                                                   4 * g:4 * (g + 1), :]

        kv_order = [2, 3, 0, 1]
        load_wgrp(lambda i: wgrp[0][:, i, :], [kv_cols(c, 2) for c in range(DC)],
                  "kv0", extra_ring_wait=[("v", v_ktab)])
        for gi in range(4):
            gidx = kv_order[gi]
            gslot = gi % 2
            if gi + 1 < 4:
                load_wgrp(lambda i, s=1 - gslot: wgrp[s][:, i, :],
                          [kv_cols(c, kv_order[gi + 1]) for c in range(DC)],
                          f"kv{gi + 1}",
                          extra_ring_wait=[("v", C[f"_wg_kv{gi}"])],
                          cast_wait=(("t", C[f"_kvp_{gi - 1}_15"]) if gi >= 1
                                     else None))
            wgready = C[f"_wg_kv{gi}"]
            for st in range(NSC):
                ppj = pp[st % 2]
                tmark = f"_kvp_{gi}_{st}"

                @block.tensor
                def _(t_, st=st, ppj=ppj, gslot=gslot, wgready=wgready, gi=gi, tmark=tmark):
                    wt(t_, "v", wgready)
                    if gi == 0 and st < 2:
                        wt(t_, "dXB", dXB_xt)
                    prev = C.get(f"_kve_{gi}_{st - 2}" if st >= 2 else
                                 f"_kve_{gi - 1}_{NSC - 2 + st}", 0)
                    if prev:
                        wt(t_, "v", prev)
                    xsrc = xT_k if st < 8 else xT_q
                    so = st if st < 8 else st - 8
                    for dc in range(DC):
                        mm = t_.matmul(ppj[:], xsrc[:, dc, 128 * so:128 * (so + 1)],
                                       wgrp[gslot][:, dc, :],
                                       start=(dc == 0), stop=(dc == DC - 1))
                    inc(mm, "t")
                    C[tmark] = C["t"]

                if gidx >= 2:
                    gg = gidx - 2

                    @block.vector
                    def _(v_, st=st, ppj=ppj, gg=gg, gi=gi, tmark=tmark):
                        wt(v_, "t", C[tmark])
                        cv = v_.tensor_copy(
                            v_arr[:, st, 4 * gg:4 * gg + 4, :].rearrange(
                                "p a b -> p (a b)"), ppj[:])
                        inc(cv, "v")
                        C[f"_kve_{gi}_{st}"] = C["v"]
                else:
                    gg = gidx

                    @block.vector
                    def _(v_, st=st, ppj=ppj, gg=gg, gi=gi, tmark=tmark):
                        wt(v_, "t", C[tmark])
                        cv = v_.tensor_copy(prj[:], ppj[:])
                        inc(cv, "v")
                        C[f"_kev_{gi}_{st}"] = C["v"]

                    @block.scalar
                    def _(s_, gi=gi, st=st):
                        wt(s_, "v", C[f"_kev_{gi}_{st}"])
                        a1 = s_.activation(rbc[:], prj[:], AF.Square)
                        inc(a1, "a")
                        C[f"_ksa_{gi}_{st}"] = C["a"]

                    @block.vector
                    def _(v_, gi=gi, st=st):
                        wt(v_, "a", C[f"_ksa_{gi}_{st}"])
                        rv = v_.tensor_reduce(stat[:, 0:4],
                                              rbc[:].rearrange("p (h x) -> p h x", x=128),
                                              axis=AX.X, op=ALU.add)
                        inc(rv, "v")
                        C[f"_krms_{gi}_{st}"] = C["v"]

                    @block.scalar
                    def _(s_, gi=gi, st=st):
                        wt(s_, "v", C[f"_krms_{gi}_{st}"])
                        a1 = s_.activation(stat[:, 4:8], stat[:, 0:4], AF.Sqrt,
                                           scale=1.0 / 128.0, bias=epsrm[:, 0:1])
                        inc(a1, "a")
                        C[f"_ksq_{gi}_{st}"] = C["a"]

                    @block.vector
                    def _(v_, st=st, gg=gg, gi=gi, ppj=ppj):
                        wt(v_, "a", C[f"_ksq_{gi}_{st}"])
                        wt(v_, "dXB")
                        VC(v_, v_.reciprocal(stat[:, 0:4], stat[:, 4:8]))
                        for hh in range(4):
                            iap = v_.tensor_scalar(prj[:, 128 * hh:128 * (hh + 1)],
                                                   prj[:, 128 * hh:128 * (hh + 1)],
                                                   stat[:, hh:hh + 1], None, ALU.mult)
                            if hh == 3:
                                VC(v_, iap)
                        x1 = prj[:].rearrange("p (h x) -> p h x", x=128)[:, :, 0:64]
                        x2 = prj[:].rearrange("p (h x) -> p h x", x=128)[:, :, 64:128]
                        o1 = rop[:].rearrange("p (h x) -> p h x", x=128)[:, :, 0:64]
                        o2 = rop[:].rearrange("p (h x) -> p h x", x=128)[:, :, 64:128]
                        tA = scr[:, 512:768].rearrange("p (h x) -> p h x", x=64)
                        tB = scr[:, 768:1024].rearrange("p (h x) -> p h x", x=64)

                        def tb(j):
                            return ktab[:, st, j, :].unsqueeze(1).broadcast_to((128, 4, 64))
                        VC(v_, v_.tensor_tensor(tA, x1, tb(0), ALU.mult))
                        VC(v_, v_.tensor_tensor(tB, x2, tb(3), ALU.mult))
                        VC(v_, v_.tensor_tensor(o1, tA, tB, ALU.subtract))
                        VC(v_, v_.tensor_tensor(tA, x2, tb(1), ALU.mult))
                        VC(v_, v_.tensor_tensor(tB, x1, tb(2), ALU.mult))
                        cv = v_.tensor_tensor(o2, tA, tB, ALU.add)
                        inc(cv, "v")
                        C[f"_kve_{gi}_{st}"] = C["v"]

                    @block.sync
                    def _(sy, st=st, gg=gg, gi=gi):
                        wt(sy, "v", C[f"_kve_{gi}_{st}"])
                        d1 = ring_dma(sy, 
                            out=kT[:, 4 * gg:4 * gg + 4, 128 * st:128 * (st + 1)],
                            in_=rop[:], transpose=True)
                        inc(d1, "dXB", 16)
        C["_kv_v"] = C["v"]
        C["_kv_t"] = C["t"]
        dXB_k = C["dXB"]

        if t_dbg is not None:
            @block.sync
            def _(sy):
                wt(sy, "v", C["_kv_v"])
                inc(ring_dma(sy, out=t_dbg[:, 0:16], in_=stat[:]), "dST", 16)
                inc(ring_dma(sy, out=t_dbg[:, 16:528], in_=prj[:]), "dST", 16)
                inc(ring_dma(sy, out=t_dbg[:, 528:1040] if False else t_dbg[:, 512:1024], in_=rbc[:]), "dST", 16)

        if t_kt is not None:
            @block.sync
            def _(sy):
                wt(sy, "dXB", dXB_k)
                inc(ring_dma(sy, out=t_kt, in_=kT[:].rearrange("p a b -> p (a b)")),
                    "dST", 16)
        if t_v is not None:
            @block.sync
            def _(sy):
                wt(sy, "v", C["_kv_v"])
                inc(ring_dma(sy, out=t_v,
                                 in_=v_arr[:].rearrange("p a b c -> p (a b c)")),
                    "dST", 16)

        if stage >= 3:
            load_wgrp(lambda i: wgrp[0][:, i, :], [q_cols(c, 0) for c in range(DC)],
                      "q0", extra_ring_wait=[("v", C["_wg_kv3"])],
                      cast_wait=("t", C["_kvp_2_15"]))
            for gq in range(4):
                gslot = gq % 2
                if gq + 1 < 4:
                    load_wgrp(lambda i, s=1 - gslot: wgrp[s][:, i, :],
                              [q_cols(c, gq + 1) for c in range(DC)], f"q{gq + 1}",
                              extra_ring_wait=[("v", C[f"_wg_q{gq}"])],
                              cast_wait=(("t", C[f"_qp_{gq - 1}_7"]) if gq >= 1
                                         else ("t", C["_kvp_3_15"])))
                wgready = C[f"_wg_q{gq}"]
                for tt in range(NTT):
                    ppj = pp[tt % 2]
                    tmark = f"_qp_{gq}_{tt}"

                    @block.tensor
                    def _(t_, tt=tt, ppj=ppj, gslot=gslot, wgready=wgready, gq=gq, tmark=tmark):
                        wt(t_, "v", wgready)
                        if gq == 0 and tt < 2:
                            wt(t_, "v", C["_kv_v"])
                        prev = C.get(f"_qe_{gq}_{tt - 2}" if tt >= 2 else
                                     f"_qe_{gq - 1}_{NTT - 2 + tt}", 0)
                        if prev:
                            wt(t_, "v", prev)
                        for dc in range(DC):
                            mm = t_.matmul(ppj[:], xT_q[:, dc, 128 * tt:128 * (tt + 1)],
                                           wgrp[gslot][:, dc, :],
                                           start=(dc == 0), stop=(dc == DC - 1))
                        inc(mm, "t")
                        C[tmark] = C["t"]

                    @block.vector
                    def _(v_, tt=tt, ppj=ppj, gq=gq, tmark=tmark):
                        wt(v_, "t", C[tmark])
                        cv = v_.tensor_copy(prj[:], ppj[:])
                        inc(cv, "v")
                        C[f"_qev_{gq}_{tt}"] = C["v"]

                    @block.scalar
                    def _(s_, gq=gq, tt=tt):
                        wt(s_, "v", C[f"_qev_{gq}_{tt}"])
                        a1 = s_.activation(rbc[:], prj[:], AF.Square)
                        inc(a1, "a")
                        C[f"_qsa_{gq}_{tt}"] = C["a"]

                    @block.vector
                    def _(v_, gq=gq, tt=tt):
                        wt(v_, "a", C[f"_qsa_{gq}_{tt}"])
                        rv = v_.tensor_reduce(stat[:, 0:4],
                                              rbc[:].rearrange("p (h x) -> p h x", x=128),
                                              axis=AX.X, op=ALU.add)
                        inc(rv, "v")
                        C[f"_qrms_{gq}_{tt}"] = C["v"]

                    @block.scalar
                    def _(s_, gq=gq, tt=tt):
                        wt(s_, "v", C[f"_qrms_{gq}_{tt}"])
                        a1 = s_.activation(stat[:, 4:8], stat[:, 0:4], AF.Sqrt,
                                           scale=1.0 / 128.0, bias=epsrm[:, 0:1])
                        inc(a1, "a")
                        C[f"_qsq_{gq}_{tt}"] = C["a"]

                    @block.vector
                    def _(v_, tt=tt, gq=gq):
                        wt(v_, "a", C[f"_qsq_{gq}_{tt}"])
                        wt(v_, "dXB")
                        VC(v_, v_.reciprocal(stat[:, 0:4], stat[:, 4:8]))
                        VC(v_, v_.tensor_scalar(stat[:, 0:4], stat[:, 0:4], ATTN_SCALE,
                                                None, ALU.mult))
                        for hh in range(4):
                            iap = v_.tensor_scalar(prj[:, 128 * hh:128 * (hh + 1)],
                                                   prj[:, 128 * hh:128 * (hh + 1)],
                                                   stat[:, hh:hh + 1], None, ALU.mult)
                            if hh == 3:
                                VC(v_, iap)
                        x1 = prj[:].rearrange("p (h x) -> p h x", x=128)[:, :, 0:64]
                        x2 = prj[:].rearrange("p (h x) -> p h x", x=128)[:, :, 64:128]
                        o1 = rop[:].rearrange("p (h x) -> p h x", x=128)[:, :, 0:64]
                        o2 = rop[:].rearrange("p (h x) -> p h x", x=128)[:, :, 64:128]
                        tA = scr[:, 512:768].rearrange("p (h x) -> p h x", x=64)
                        tB = scr[:, 768:1024].rearrange("p (h x) -> p h x", x=64)

                        def tb(j):
                            return qtab[:, tt, j, :].unsqueeze(1).broadcast_to((128, 4, 64))
                        VC(v_, v_.tensor_tensor(tA, x1, tb(0), ALU.mult))
                        VC(v_, v_.tensor_tensor(tB, x2, tb(3), ALU.mult))
                        VC(v_, v_.tensor_tensor(o1, tA, tB, ALU.subtract))
                        VC(v_, v_.tensor_tensor(tA, x2, tb(1), ALU.mult))
                        VC(v_, v_.tensor_tensor(tB, x1, tb(2), ALU.mult))
                        cv = v_.tensor_tensor(o2, tA, tB, ALU.add)
                        inc(cv, "v")
                        C[f"_qe_{gq}_{tt}"] = C["v"]

                    @block.sync
                    def _(sy, tt=tt, gq=gq):
                        wt(sy, "v", C[f"_qe_{gq}_{tt}"])
                        d1 = ring_dma(sy, 
                            out=qT[:, 4 * gq:4 * gq + 4, 128 * tt:128 * (tt + 1)],
                            in_=rop[:], transpose=True)
                        inc(d1, "dXB", 16)
            C["_q_v"] = C["v"]
            dXB_q = C["dXB"]

            if t_qt is not None:
                @block.sync
                def _(sy):
                    wt(sy, "dXB", dXB_q)
                    inc(ring_dma(sy, out=t_qt, in_=qT[:].rearrange("p a b -> p (a b)")),
                        "dST", 16)

        if stage >= 4:
            @block.gpsimd
            def _(g_):
                g_.wait_ge(SEM["v"], C["_q_v"])
                inc(g_.iota(scr[:, 512:1024].bitcast(I32)[:, 0:512],
                            pattern=[[1, 512]], base=0, channel_multiplier=0), "p")
            p_iota2 = C["p"]

            @block.vector
            def _(v_):
                wt(v_, "v", C["_q_v"])
                wt(v_, "p", p_iota2)
                dmask = scr[:, 0:512]
                VC(v_, v_.tensor_scalar(dmask, scr[:, 512:1024].bitcast(I32)[:, 0:512],
                                        piof[:, 0:1], None, ALU.subtract))
                for k in range(4):
                    v_.tensor_scalar(masks[:, k, :], dmask, float(128 * k),
                                     None, ALU.is_lt)
                for k in range(8, 12):
                    mk = v_.tensor_scalar(masks[:, k - 4, :], dmask,
                                          float(128 * k - 1024), None, ALU.is_ge)
                VC(v_, mk)
                iv4 = v_.tensor_copy(scr[:, 512:513], piof[:])
                inc(iv4, "v")
            v_masks = C["v"]

            for g in range(2):
                for n in range(NH):
                    pe = pp[2 + (n % 2)]
                    smps = pp[4 + (n % 2)]
                    for k in range(12):
                        c = 4 * g + k
                        sp = pp[k % 2]
                        smark = f"_s_{g}_{n}_{k}"

                        @block.tensor
                        def _(t_, g=g, n=n, k=k, c=c, sp=sp, smark=smark):
                            if g == 0 and n == 0 and k == 0:
                                wt(t_, "dXB", C["dXB"])
                                wt(t_, "v", v_masks)
                            prev = C.get(f"_ex_{g}_{n}_{k - 2}" if k >= 2 else
                                         (f"_ex_{g}_{n - 1}_{10 + k}" if n >= 1 else
                                          (f"_ex_{g - 1}_{NH - 1}_{10 + k}" if g >= 1
                                           else "_none")), 0)
                            if prev:
                                wt(t_, "a", prev)
                            mm = t_.matmul(sp[:], kT[:, n // 2, 128 * c:128 * (c + 1)],
                                           qT[:, n, 512 * g:512 * (g + 1)],
                                           start=True, stop=True)
                            inc(mm, "t")
                            C[smark] = C["t"]

                        @block.scalar
                        def _(s_, g=g, n=n, k=k, c=c, sp=sp, smark=smark):
                            wt(s_, "t", C[smark])
                            if k == 0:
                                prevpv = C.get(f"_pv_{g}_{n - 1}" if n >= 1 else
                                               (f"_pv_{g - 1}_{NH - 1}" if g >= 1
                                                else "_none"), 0)
                                if prevpv:
                                    wt(s_, "t", prevpv)
                            a1 = s_.activation(ptil[:, k, :], sp[:], AF.Exp,
                                               bias=valv[:, c:c + 1], scale=1.0)
                            inc(a1, "a")
                            C[f"_ex_{g}_{n}_{k}"] = C["a"]

                    @block.vector
                    def _(v_, g=g, n=n):
                        wt(v_, "a", C[f"_ex_{g}_{n}_11"])
                        for k in (0, 1, 2, 3):
                            v_.tensor_tensor(ptil[:, k, :], ptil[:, k, :],
                                             masks[:, k, :], ALU.mult)
                        for k in (8, 9, 10, 11):
                            mv = v_.tensor_tensor(ptil[:, k, :], ptil[:, k, :],
                                                  masks[:, k - 4, :], ALU.mult)
                        inc(mv, "v")
                        C[f"_msk_{g}_{n}"] = C["v"]

                    @block.tensor
                    def _(t_, g=g, n=n, pe=pe, smps=smps):
                        wt(t_, "v", C[f"_msk_{g}_{n}"])
                        prev = C.get(f"_enc_{g}_{n - 2}" if n >= 2 else
                                     (f"_enc_{g - 1}_{NH - 2 + n}" if g >= 1
                                      else "_none"), 0)
                        if prev:
                            wt(t_, "v", prev)
                        for k in range(12):
                            c = 4 * g + k
                            t_.matmul(pe[:], v_arr[:, c, n // 2, :], ptil[:, k, :],
                                      start=(k == 0), stop=(k == 11))
                        for k in range(12):
                            mm = t_.matmul(smps[0:1, :], onesb[:], ptil[:, k, :],
                                           start=(k == 0), stop=(k == 11))
                        inc(mm, "t")
                        C[f"_pv_{g}_{n}"] = C["t"]

                    @block.vector
                    def _(v_, g=g, n=n, smps=smps):
                        wt(v_, "t", C[f"_pv_{g}_{n}"])
                        VC(v_, v_.tensor_copy(scr[0:1, 0:512], smps[0:1, :]))
                        rv = v_.reciprocal(scr[0:1, 512:1024], scr[0:1, 0:512])
                        inc(rv, "v")
                        C[f"_rc_{g}_{n}"] = C["v"]

                    @block.tensor
                    def _(t_, g=g, n=n):
                        wt(t_, "v", C[f"_rc_{g}_{n}"])
                        prev = C.get(f"_enc_{g}_{n - 1}" if n >= 1 else
                                     (f"_enc_{g - 1}_{NH - 1}" if g >= 1 else "_none"), 0)
                        if prev:
                            wt(t_, "v", prev)
                        mm = t_.matmul(pp_r[:], onesf1[:], scr[0:1, 512:1024],
                                       start=True, stop=True)
                        inc(mm, "t")
                        C[f"_bc_{g}_{n}"] = C["t"]

                    @block.vector
                    def _(v_, g=g, n=n, pe=pe):
                        wt(v_, "t", C[f"_bc_{g}_{n}"])
                        VC(v_, v_.tensor_copy(rbc[:], pp_r[:]))
                        cv = v_.tensor_tensor(enc[:, n, 512 * g:512 * (g + 1)],
                                              pe[:], rbc[:], ALU.mult)
                        inc(cv, "v")
                        C[f"_enc_{g}_{n}"] = C["v"]
            C["_attn_v"] = C["v"]
            C["_attn_t"] = C["t"]

            if t_enc is not None:
                @block.sync
                def _(sy):
                    wt(sy, "v", C["_attn_v"])
                    inc(ring_dma(sy, out=t_enc,
                                     in_=enc[:].rearrange("p a b -> p (a b)")),
                        "dST", 16)

        if stage >= 5:
            for tt in range(NTT):
                @block.sync
                def _(sy, tt=tt):
                    if tt == 0:
                        wt(sy, "t", C["_attn_t"])
                    d1 = ring_dma(sy, out=e_t[tt][:],
                                      in_=x_kv[TQ + 128 * tt:TQ + 128 * (tt + 1), :])
                    inc(d1, "dC", 16)
            dC_x = C["dC"]

            for dg in range(4):
                gslot = dg % 2
                for n in range(NH):
                    wsl = n % 2

                    @block.sync
                    def _(sy, dg=dg, n=n, wsl=wsl):
                        if dg == 0 and n == 0:
                            wt(sy, "t", C["_attn_t"])
                        if n >= 2 or dg >= 1:
                            wt(sy, "v", C[f"_oc_{dg}_{n - 2}" if n >= 2 else
                                          f"_oc_{dg - 1}_{NH - 2 + n}"])
                        d1 = ring_dma(sy, out=wstg[:, 512 * wsl:512 * (wsl + 1)],
                                          in_=okw[n, :, 512 * dg:512 * (dg + 1)])
                        inc(d1, "dW%d" % wsl, 16)

                    @block.vector
                    def _(v_, dg=dg, n=n, wsl=wsl, gslot=gslot):
                        wt(v_, "dW%d" % wsl, C["dW%d" % wsl])
                        if dg == 0 and n == 0:
                            wt(v_, "t", C["_attn_t"])
                        if dg >= 2 and n == 0:
                            wt(v_, "t", C[f"_op_{dg - 2}_7"])
                        cv = v_.tensor_copy(wo_grp[gslot][:, n, :],
                                            wstg[:, 512 * wsl:512 * (wsl + 1)])
                        inc(cv, "v")
                        C[f"_oc_{dg}_{n}"] = C["v"]
                wgready = C[f"_oc_{dg}_{NH - 1}"]
                for tt in range(NTT):
                    ppj = pp[tt % 2]
                    tmark = f"_op_{dg}_{tt}"

                    @block.tensor
                    def _(t_, dg=dg, tt=tt, ppj=ppj, gslot=gslot, wgready=wgready, tmark=tmark):
                        wt(t_, "v", wgready)
                        prev = C.get(f"_oe_{dg}_{tt - 2}" if tt >= 2 else
                                     (f"_oe_{dg - 1}_{NTT - 2 + tt}" if dg >= 1
                                      else "_none"), 0)
                        if prev:
                            wt(t_, "v", prev)
                        for n in range(NH):
                            mm = t_.matmul(ppj[:], enc[:, n, 128 * tt:128 * (tt + 1)],
                                           wo_grp[gslot][:, n, :],
                                           start=(n == 0), stop=(n == NH - 1))
                        inc(mm, "t")
                        C[tmark] = C["t"]

                    @block.vector
                    def _(v_, dg=dg, tt=tt, ppj=ppj, tmark=tmark):
                        wt(v_, "t", C[tmark])
                        if dg == 0 and tt == 0:
                            wt(v_, "dC", dC_x)
                        cv = v_.tensor_tensor(e_t[tt][:, 512 * dg:512 * (dg + 1)],
                                              e_t[tt][:, 512 * dg:512 * (dg + 1)],
                                              ppj[:], ALU.add)
                        inc(cv, "v")
                        C[f"_oe_{dg}_{tt}"] = C["v"]
            C["_op_t"] = C["t"]

            for tt in range(NTT):
                @block.scalar
                def _(s_, tt=tt):
                    s_.wait_ge(SEM["v"], C[f"_oe_3_{tt}"])
                    if tt >= 1:
                        s_.wait_ge(SEM["v"], C[f"_l1s_{tt - 1}"])
                    a1 = s_.activation(lnscr[:], e_t[tt][:], AF.Square)
                    inc(a1, "a")
                    C[f"_l1q_{tt}"] = C["a"]

                @block.vector
                def _(v_, tt=tt):
                    wt(v_, "v", C[f"_oe_3_{tt}"])
                    wt(v_, "a", C[f"_l1q_{tt}"])
                    et = e_t[tt][:]
                    VC(v_, v_.tensor_reduce(stat[:, 0:1], et, axis=AX.X, op=ALU.add))
                    VC(v_, v_.tensor_reduce(stat[:, 1:2], lnscr[:], axis=AX.X, op=ALU.add))
                    VC(v_, v_.tensor_scalar(stat[:, 0:1], stat[:, 0:1], 1.0 / D, None, ALU.mult))
                    VC(v_, v_.tensor_scalar(stat[:, 1:2], stat[:, 1:2], 1.0 / D, None, ALU.mult))
                    VC(v_, v_.tensor_scalar(stat[:, 2:3], stat[:, 0:1], stat[:, 0:1],
                                            None, ALU.mult))
                    cv = v_.tensor_tensor(stat[:, 1:2], stat[:, 1:2], stat[:, 2:3],
                                          ALU.subtract)
                    inc(cv, "v")
                    C[f"_l1s_{tt}"] = C["v"]

                @block.scalar
                def _(s_, tt=tt):
                    wt(s_, "v", C[f"_l1s_{tt}"])
                    aln = s_.activation(stat[:, 2:3], stat[:, 1:2], AF.Ln, bias=epsln[:, 0:1])
                    inc(aln, "a")
                    wt(s_, "a")
                    a1 = s_.activation(stat[:, 3:4], stat[:, 2:3], AF.Exp, scale=-0.5)
                    inc(a1, "a")
                    C[f"_l1r_{tt}"] = C["a"]

                @block.vector
                def _(v_, tt=tt):
                    wt(v_, "a", C[f"_l1r_{tt}"])
                    et = e_t[tt][:]
                    v_.tensor_scalar(et, et, stat[:, 0:1], stat[:, 3:4],
                                     ALU.subtract, ALU.mult)
                    ev = et.rearrange("p (a two) -> p two a", two=2)
                    if tt >= 1:
                        wt(v_, "dXB", 16 * 0 + C["dXB"])
                    v_.tensor_copy(yev[:, 0:1024], ev[:, 0, :])
                    cv = v_.tensor_copy(yev[:, 1024:2048], ev[:, 1, :])
                    inc(cv, "v")
                    C[f"_ysp_{tt}"] = C["v"]

                @block.sync
                def _(sy, tt=tt):
                    wt(sy, "v", C[f"_ysp_{tt}"])
                    if tt == 0:
                        wt(sy, "t", C["_op_t"])
                    d0 = ring_dma(sy, out=yTe[:, :, 128 * tt:128 * (tt + 1)],
                                      in_=yev[:, 0:1024], transpose=True)
                    inc(d0, "dXB", 16)
                    d1 = ring_dma(sy, out=yTo[:, :, 128 * tt:128 * (tt + 1)],
                                      in_=yev[:, 1024:2048], transpose=True)
                    inc(d1, "dXB", 16)
            C["_y_xb"] = C["dXB"]
            C["_ln1_v"] = C["v"]

            if t_y is not None:
                for tt in range(NTT):
                    @block.sync
                    def _(sy, tt=tt):
                        wt(sy, "v", C["_ln1_v"])
                        inc(ring_dma(sy, out=t_y[128 * tt:128 * (tt + 1), :],
                                         in_=e_t[tt][:]), "dST", 16)

        if stage >= 6:
            LUT_SPEC = [
                (7, 6, 12),
                (0x70, 2, 8),
            ]

            def gu_dequant(v_, s_):
                sl4 = s_ % 4
                psl = s_ % 2
                wt(v_, "dPK%d" % (s_ % 2), 64 * (s_ // 2 + 1))
                if s_ >= 2:
                    wt(v_, "dT%d" % psl, 16 * (s_ // 2))
                w = gupk[sl4][:].rearrange("p a b -> p (a b)")
                T, S = dscT, dscS
                fv = T.bitcast(BF16)
                fv3 = fv.rearrange("p (m w) -> p m w", m=2)
                S3 = S.rearrange("p (m w) -> p m w", m=2)
                for ni, (msk, shl, sshl) in enumerate(LUT_SPEC):
                    gb_bf = plv_bf[psl][:, :, ni, :]
                    gb_u = plv_u16[psl][:, :, ni, :]
                    v_.tensor_scalar(T, w, msk, shl, ALU.bitwise_and,
                                     ALU.logical_shift_left)
                    v_.tensor_scalar(T, T, 16128, None, ALU.add)
                    v_.tensor_scalar(S, w, sshl, 0x8000, ALU.logical_shift_left,
                                     ALU.bitwise_and)
                    v_.tensor_scalar(gb_bf, fv3, 1.0, 0.0, ALU.subtract, ALU.min)
                    v_.tensor_tensor(gb_bf, gb_bf, fv3, ALU.add)
                    v_.tensor_tensor(gb_u, gb_u, S3, ALU.bitwise_xor)
                    for m in range(2):
                        sca = gscf[:, 2 * sl4 + m, :].unsqueeze(2).broadcast_to(
                            (128, 64, 16))
                        pm = plv_bf[psl][:, m, ni, :].rearrange(
                            "p (b r) -> p b r", r=16)
                        last = v_.tensor_tensor(pm, pm, sca, ALU.mult)
                inc(last, "v")
                C[f"_dq_{s_}"] = C["v"]

            def dn_dequant(v_, k):
                dsl = 2 if (k % GSETS) == 5 else (k % 2)
                wt(v_, "dDN%d" % (k % 2), 64 * (k // 2 + 1))
                if k >= 1:
                    wt(v_, "dU%d" % (((k - 1) // 2) % 2))
                w = dnpk[dsl][:].rearrange("p a b -> p (a b)")
                T, S = dscT[:, 0:1024], dscS[:, 0:1024]
                fv = T.bitcast(BF16)
                fv3 = fv.rearrange("p (m w) -> p m w", m=2)
                S3 = S.rearrange("p (m w) -> p m w", m=2)
                for ni, (msk, shl, sshl) in enumerate(LUT_SPEC):
                    gb_bf = dpl_bf[:, :, ni, :]
                    gb_u = dpl_u16[:, :, ni, :]
                    v_.tensor_scalar(T, w, msk, shl, ALU.bitwise_and,
                                     ALU.logical_shift_left)
                    v_.tensor_scalar(T, T, 16128, None, ALU.add)
                    v_.tensor_scalar(S, w, sshl, 0x8000, ALU.logical_shift_left,
                                     ALU.bitwise_and)
                    v_.tensor_scalar(gb_bf, fv3, 1.0, 0.0, ALU.subtract, ALU.min)
                    v_.tensor_tensor(gb_bf, gb_bf, fv3, ALU.add)
                    v_.tensor_tensor(gb_u, gb_u, S3, ALU.bitwise_xor)
                    for i2 in range(2):
                        sca = dscf[:, 2 * dsl + i2, :].unsqueeze(2).broadcast_to(
                            (128, 32, 16))
                        pm = dpl_bf[:, i2, ni, :].rearrange("p (b r) -> p b r", r=16)
                        last = v_.tensor_tensor(pm, pm, sca, ALU.mult)
                inc(last, "v")
                C[f"_dndq_{k}"] = C["v"]

            g_pk2 = g_pk.rearrange("(r two) w -> two r w", two=2)
            u_pk2 = u_pk.rearrange("(r two) w -> two r w", two=2)
            g_sc2 = g_sc.rearrange("(r two) w -> two r w", two=2)
            u_sc2 = u_sc.rearrange("(r two) w -> two r w", two=2)
            glsl = [glscr,
        qtab[:].rearrange("p a b c -> p (a b c)")[:, 0:256].bitcast(BF16)]

            def gu_loads(sy, t):
                slt = t % 4
                part, jt = (t % GSETS) % 2, (t % GSETS) // 2
                r0h = 512 * (t // GSETS) + 128 * jt
                sem = "dPK%d" % (t % 2)
                ring_dma(sy, out=gupk[slt][:, 0, :],
                         in_=g_pk2[part, r0h:r0h + 128, :]).then_inc(SEM[sem], 16)
                ring_dma(sy, out=gupk[slt][:, 1, :],
                         in_=u_pk2[part, r0h:r0h + 128, :]).then_inc(SEM[sem], 16)
                ring_dma(sy, out=gscf[:, 2 * slt + 0, :],
                         in_=g_sc2[part, r0h:r0h + 128, :]).then_inc(SEM[sem], 16)
                d1 = ring_dma(sy, out=gscf[:, 2 * slt + 1, :],
                              in_=u_sc2[part, r0h:r0h + 128, :])
                inc(d1, sem, 16)
                C[sem] += 48

            def dn_loads(sy, k):
                dsl = 2 if (k % GSETS) == 5 else (k % 2)
                sem = "dDN%d" % (k % 2)
                gi_ = k // GSETS
                hg_ = (k % GSETS) // 2
                hc0 = 4 * hg_ + 2 * (k % 2)
                for i2 in range(2):
                    hc = hc0 + i2
                    ring_dma(sy, out=dnpk[dsl][:, i2, :],
                             in_=d_pk[128 * hc:128 * (hc + 1),
                                      512 * gi_:512 * (gi_ + 1)]).then_inc(SEM[sem], 16)
                    d1 = ring_dma(sy, out=dscf[:, 2 * dsl + i2, :],
                                  in_=d_sc[128 * hc:128 * (hc + 1),
                                           32 * gi_:32 * (gi_ + 1)])
                    if i2 == 1:
                        inc(d1, sem, 16)
                        C[sem] += 48
                    else:
                        d1.then_inc(SEM[sem], 16)

            def dn_xbar_section(k):
                @block.sync
                def _(sy, k=k):
                    wt(sy, "v", C[f"_dndq_{k}"])
                    hgg = k // 2
                    ws2 = hgg % 2
                    sbr = k % 2
                    if hgg >= 2 and sbr == 0:
                        wt(sy, "t", C.get(f"_dmm_{hgg - 2}", 0))
                    for i2 in range(2):
                        hcL = 2 * sbr + i2
                        d1 = ring_dma(sy,
                                      out=WdT[ws2][:, :, 128 * hcL:128 * (hcL + 1)],
                                      in_=dplanes[:, i2, :], transpose=True)
                        if i2 == 1:
                            inc(d1, "dU%d" % ws2, 16)
                            C["dU%d" % ws2] += 16
                        else:
                            d1.then_inc(SEM["dU%d" % ws2], 16)

            for gi in range(NGRP):
                dn_on = gi in (DOWN_GRPS if DOWN_GRPS is not None else range(NGRP))
                for slo in range(GSETS):
                    s_ = GSETS * gi + slo
                    psl = s_ % 2
                    ws = s_ % 2

                    if s_ == 0:
                        @block.sync
                        def _(sy):
                            wt(sy, "v", C["_ln1_v"])
                            wt(sy, "dXB", C["_y_xb"])
                            for t in range(4):
                                gu_loads(sy, t)

                    @block.vector
                    def _(v_, s_=s_):
                        gu_dequant(v_, s_)

                    @block.sync
                    def _(sy, s_=s_, psl=psl, ws=ws, dn_on=dn_on):
                        wt(sy, "v", C[f"_dq_{s_}"])
                        if s_ + 4 < GSETS * NGRP:
                            gu_loads(sy, s_ + 4)
                        if dn_on:
                            dn_loads(sy, s_)
                        if s_ >= 2:
                            wt(sy, "t", C[f"_gu_{s_ - 2}_1"])
                        d1 = ring_dma(sy, out=WT[ws][:],
                                          in_=planes[psl][:].rearrange("p a b -> p (a b)"),
                                          transpose=True)
                        inc(d1, "dT%d" % ws, 16)

                    if dn_on and slo <= 4:
                        @block.vector
                        def _(v_, s_=s_):
                            dn_dequant(v_, s_)

                    if dn_on and slo <= 3:
                        dn_xbar_section(s_)

                    for th in range(2):
                        u = 2 * s_ + th
                        pg = pp[2 * (u % 3)]
                        pu = pp[2 * (u % 3) + 1]

                        @block.tensor
                        def _(t_, s_=s_, th=th, u=u, ws=ws, pg=pg, pu=pu, gi=gi):
                            if th == 0:
                                wt(t_, "dT%d" % ws, 16 * (s_ // 2 + 1))
                                if s_ == 0:
                                    wt(t_, "dXB", C["_y_xb"])
                            prev = C.get(f"_ie_{u - 3}", 0)
                            if prev:
                                wt(t_, "v", prev)
                            for jj in range(8):
                                t_.matmul(pg[:], WT[ws][:, jj, :],
                                          yTe[:, jj, 512 * th:512 * (th + 1)],
                                          start=(jj == 0), stop=False)
                            for jj in range(8):
                                t_.matmul(pg[:], WT[ws][:, 8 + jj, :],
                                          yTo[:, jj, 512 * th:512 * (th + 1)],
                                          start=False, stop=(jj == 7))
                            for jj in range(8):
                                t_.matmul(pu[:], WT[ws][:, 16 + jj, :],
                                          yTe[:, jj, 512 * th:512 * (th + 1)],
                                          start=(jj == 0), stop=False)
                            for jj in range(8):
                                mm = t_.matmul(pu[:], WT[ws][:, 24 + jj, :],
                                               yTo[:, jj, 512 * th:512 * (th + 1)],
                                               start=False, stop=(jj == 7))
                            inc(mm, "t")
                            C[f"_gu_{s_}_{th}"] = C["t"]

                        @block.scalar
                        def _(s2, s_=s_, th=th, u=u, pg=pg):
                            wt(s2, "t", C[f"_gu_{s_}_{th}"])
                            prev = C.get(f"_ie_{u - 2}", 0)
                            if prev:
                                wt(s2, "v", prev)
                            a1 = s2.activation(glsl[u % 2], pg[:], AF.Gelu_apprx_tanh)
                            inc(a1, "a")
                            C[f"_gl_{u}"] = C["a"]

                        @block.vector
                        def _(v_, s_=s_, th=th, u=u, slo=slo, pu=pu, gi=gi):
                            wt(v_, "a", C[f"_gl_{u}"])
                            if gi >= 1 and u % (2 * GSETS) < 2:
                                wt(v_, "t", C.get(f"_dn_t_{gi - 1}", 0))
                            cv = v_.tensor_tensor(
                                inter[:, slo, 512 * th:512 * (th + 1)],
                                glsl[u % 2], pu[:], ALU.mult)
                            inc(cv, "v")
                            C[f"_ie_{u}"] = C["v"]

                if t_int is not None and gi == 0:
                    @block.sync
                    def _(sy, gi=gi):
                        wt(sy, "v", C[f"_ie_{2 * GSETS - 1}"])
                        inc(ring_dma(sy, out=t_int,
                                         in_=inter[:].rearrange("p a b -> p (a b)")),
                            "dST", 16)

                if not dn_on:
                    C[f"_dn_t_{gi}"] = C["t"]
                    for tt in range(NTT):
                        C[f"_de_{gi}_3_{tt}"] = C["v"]
                    continue
                for hg in range(4):
                    hgg = 4 * gi + hg
                    ws2 = hgg % 2
                    for tt in range(NTT):
                        @block.tensor
                        def _(t_, gi=gi, hg=hg, hgg=hgg, ws2=ws2, tt=tt):
                            if tt == 0:
                                wt(t_, "dU%d" % ws2, 64 * (hgg // 2 + 1))
                                wt(t_, "v", C[f"_ie_{2 * GSETS * (gi + 1) - 1}"])
                            if DBG_CB is not None and gi == 0 and hg == 0 and tt == 0:
                                t_.dbg_aps(DBG_CB, inter[:, :, 0:128],
                                           WdT[ws2][:])
                            pd = (pp_r, pp6)[tt % 2]
                            prev = C.get(f"_de_{gi}_{hg}_{tt - 2}" if tt >= 2 else
                                         (f"_de_{gi}_{hg - 1}_{NTT - 2 + tt}" if hg >= 1 else
                                          (f"_de_{gi - 1}_3_{NTT - 2 + tt}" if gi >= 1
                                           else "_none")), 0)
                            if prev:
                                t_.wait_ge(SEM["v"], prev)
                            for slo in range(GSETS):
                                par, j = slo % 2, slo // 2
                                mm = t_.matmul(pd[:], inter[:, slo, 128 * tt:128 * (tt + 1)],
                                               WdT[ws2][:, 4 * par + j, :],
                                               start=(slo == 0), stop=(slo == GSETS - 1))
                            inc(mm, "t")
                            C[f"_dmmt_{hgg}_{tt}"] = C["t"]
                            if DBG_CB is not None and gi == 0 and hg == 0 and tt == 0:
                                t_.dbg_aps(DBG_CB, pd[:])

                        @block.vector
                        def _(v_, gi=gi, hg=hg, hgg=hgg, tt=tt):
                            pd = (pp_r, pp6)[tt % 2]
                            wt(v_, "t", C[f"_dmmt_{hgg}_{tt}"])
                            if DBG_CB is not None and gi == 0 and hg == 0 and tt == 0:
                                v_.dbg_aps(DBG_CB, pd[:], e_t[tt][:, 0:128])
                            cv = v_.tensor_tensor(e_t[tt][:, 512 * hg:512 * (hg + 1)],
                                                  e_t[tt][:, 512 * hg:512 * (hg + 1)],
                                                  pd[:], ALU.add)
                            inc(cv, "v")
                            C[f"_de_{gi}_{hg}_{tt}"] = C["v"]
                    C[f"_dmm_{hgg}"] = C["t"]
                    if hg < 2:
                        k1 = GSETS * gi + 4 + 2 * hg
                        dn_xbar_section(k1)
                        for kq in (k1 + 1, k1 + 2):
                            if kq < GSETS * gi + GSETS:
                                @block.vector
                                def _(v_, k=kq):
                                    dn_dequant(v_, k)
                                if kq == k1 + 1:
                                    dn_xbar_section(kq)
                C[f"_dn_t_{gi}"] = C["t"]

        if t_e2 is not None and stage >= 6:
            for tt in range(NTT):
                @block.sync
                def _(sy, tt=tt):
                    wt(sy, "v", C[f"_de_{NGRP - 1}_3_{tt}"])
                    inc(ring_dma(sy, out=t_e2[128 * tt:128 * (tt + 1), :],
                                 in_=e_t[tt][:]), "dST", 16)

        if stage >= 7:
            for tt in range(NTT):
                @block.scalar
                def _(s_, tt=tt):
                    s_.wait_ge(SEM["v"], C[f"_de_{NGRP - 1}_3_{tt}"])
                    if tt >= 1:
                        s_.wait_ge(SEM["v"], C[f"_l2s_{tt - 1}"])
                    a1 = s_.activation(lnscr[:], e_t[tt][:], AF.Square)
                    inc(a1, "a")
                    C[f"_l2q_{tt}"] = C["a"]

                @block.vector
                def _(v_, tt=tt):
                    wt(v_, "v", C[f"_de_{NGRP - 1}_3_{tt}"])
                    wt(v_, "a", C[f"_l2q_{tt}"])
                    et = e_t[tt][:]
                    VC(v_, v_.tensor_reduce(stat[:, 0:1], et, axis=AX.X, op=ALU.add))
                    VC(v_, v_.tensor_reduce(stat[:, 1:2], lnscr[:], axis=AX.X, op=ALU.add))
                    VC(v_, v_.tensor_scalar(stat[:, 0:1], stat[:, 0:1], 1.0 / D, None, ALU.mult))
                    VC(v_, v_.tensor_scalar(stat[:, 1:2], stat[:, 1:2], 1.0 / D, None, ALU.mult))
                    VC(v_, v_.tensor_scalar(stat[:, 2:3], stat[:, 0:1], stat[:, 0:1],
                                            None, ALU.mult))
                    cv = v_.tensor_tensor(stat[:, 1:2], stat[:, 1:2], stat[:, 2:3],
                                          ALU.subtract)
                    inc(cv, "v")
                    C[f"_l2s_{tt}"] = C["v"]

                @block.scalar
                def _(s_, tt=tt):
                    wt(s_, "v", C[f"_l2s_{tt}"])
                    aln = s_.activation(stat[:, 2:3], stat[:, 1:2], AF.Ln, bias=epsln[:, 0:1])
                    inc(aln, "a")
                    wt(s_, "a")
                    a1 = s_.activation(stat[:, 3:4], stat[:, 2:3], AF.Exp, scale=-0.5)
                    inc(a1, "a")
                    C[f"_l2r_{tt}"] = C["a"]

                @block.vector
                def _(v_, tt=tt):
                    wt(v_, "a", C[f"_l2r_{tt}"])
                    cv = v_.tensor_scalar(e_t[tt][:], e_t[tt][:], stat[:, 0:1],
                                          stat[:, 3:4], ALU.subtract, ALU.mult)
                    inc(cv, "v")
                    C[f"_l2a_{tt}"] = C["v"]

                @block.sync
                def _(sy, tt=tt):
                    wt(sy, "v", C[f"_l2a_{tt}"])
                    d1 = ring_dma(sy, out=out[128 * tt:128 * (tt + 1), :],
                                      in_=e_t[tt][:])
                    inc(d1, "dST", 16)
        else:
            @block.sync
            def _(sy):
                for tt in range(NTT):
                    d1 = ring_dma(sy, out=out[128 * tt:128 * (tt + 1), :],
                                      in_=e_t[tt][:])
                    inc(d1, "dST", 16)

        @block.sync
        def _(sy):
            wt(sy, "dST")

    return nc, tap_aps


def _shards(inputs):
    x = np.asarray(inputs["L0_attn_x"])
    qkw = np.ascontiguousarray(np.asarray(inputs["L0_attn_q_kernel"]))
    kvw = np.ascontiguousarray(np.asarray(inputs["L0_attn_kv_kernel"]))
    okw = np.ascontiguousarray(np.asarray(inputs["L0_attn_out_kernel"]))
    qnsv = np.ascontiguousarray(np.asarray(inputs["L0_attn_q_norm_scale"])).reshape(1, HD)
    knsv = np.ascontiguousarray(np.asarray(inputs["L0_attn_k_norm_scale"])).reshape(1, HD)
    gpk = np.asarray(inputs["L0_ffn_gate_W_packed"]).astype(np.uint16)
    gsc = np.ascontiguousarray(np.asarray(inputs["L0_ffn_gate_scales"]))
    upk = np.asarray(inputs["L0_ffn_up_W_packed"]).astype(np.uint16)
    usc = np.ascontiguousarray(np.asarray(inputs["L0_ffn_up_scales"]))
    dpk = np.asarray(inputs["L0_ffn_down_W_packed"]).astype(np.uint16)
    dsc = np.ascontiguousarray(np.asarray(inputs["L0_ffn_down_scales"]))
    invts = np.ascontiguousarray(np.broadcast_to(
        (1.0 / (10000.0 ** (np.arange(64, dtype=np.float64) / 64.0))
         ).astype(np.float32)[None, :], (128, 64)))
    maps = []
    for core in range(8):
        b, h = core // 2, core % 2
        if h == 0:
            xkv = np.concatenate([np.zeros((1024, D), np.float32), x[b, 0:1024]],
                                 axis=0)
            val = np.concatenate([np.full(1024, -1e30, np.float32),
                                  np.zeros(1024, np.float32)])
            kof = np.full(128, -1024.0, np.float32)
        else:
            xkv = np.ascontiguousarray(x[b])
            val = np.zeros(2048, np.float32)
            kof = np.zeros(128, np.float32)
        maps.append({
            "x_kv": xkv, "val_in": val, "koff_in": kof, "invts_in": invts,
            "qkw": qkw, "kvk": kvw, "okw": okw, "qns": qnsv, "kns": knsv,
            "g_pk": gpk, "g_sc": gsc, "u_pk": upk, "u_sc": usc,
            "d_pk": dpk, "d_sc": dsc,
        })
    return maps


def kernel(**inputs):
    maps = _shards(inputs)
    nc, _ = build_program(taps=TAPS, stage=STAGE)
    res = run_bass_kernel_spmd(nc, maps, core_ids=list(range(8)))
    outp = np.empty((4, 2048, 2048), np.float32)
    for core in range(8):
        b, h = core // 2, core % 2
        outp[b, 1024 * h:1024 * (h + 1), :] = res.results[core]["out"]
    return outp


# version 03
# speedup vs baseline: 1.2603x; 1.2603x over previous
import numpy as np
import ml_dtypes
import concourse.bass as bass
import concourse.mybir as mybir
from concourse.bass_utils import run_bass_kernel_spmd
from contextlib import ExitStack

F32 = mybir.dt.float32
BF16 = mybir.dt.bfloat16
U16 = mybir.dt.uint16
I32 = mybir.dt.int32
AF = mybir.ActivationFunctionType
ALU = mybir.AluOpType
AX = mybir.AxisListType

TQ, TK, D = 1024, 2048, 2048
NH, KH, HD = 16, 8, 128
DC, NTT, NSC = 16, 8, 16
MLP = 16384
GSETS = 8
NGRP = 16
ATTN_SCALE = 0.08838834764831845
TWO_PI = float(2 * np.pi)
HALF_PI = float(np.pi / 2)

TAPS = ()
STAGE = 99
DOWN_GRPS = None
DBG_CB = None

K1 = 1024


def build_program(taps=(), stage=99):
    nc = bass.Bass("TRN2", target_bir_lowering=False, debug=(DBG_CB is not None))

    x_kv = nc.dram_tensor("x_kv", [TK, D], F32, kind="ExternalInput").ap()
    val_in = nc.dram_tensor("val_in", [TK], F32, kind="ExternalInput").ap()
    koff_in = nc.dram_tensor("koff_in", [128], F32, kind="ExternalInput").ap()
    invts_in = nc.dram_tensor("invts_in", [128, 64], F32, kind="ExternalInput").ap()
    qkw = nc.dram_tensor("qkw", [NH, D, HD], F32, kind="ExternalInput").ap()
    kvk = nc.dram_tensor("kvk", [2, KH, D, HD], F32, kind="ExternalInput").ap()
    okw = nc.dram_tensor("okw", [NH, HD, D], F32, kind="ExternalInput").ap()
    qns = nc.dram_tensor("qns", [1, HD], F32, kind="ExternalInput").ap()
    kns = nc.dram_tensor("kns", [1, HD], F32, kind="ExternalInput").ap()
    g_pk = nc.dram_tensor("g_pk", [MLP, D // 2], U16, kind="ExternalInput").ap()
    g_sc = nc.dram_tensor("g_sc", [MLP, 64], F32, kind="ExternalInput").ap()
    u_pk = nc.dram_tensor("u_pk", [MLP, D // 2], U16, kind="ExternalInput").ap()
    u_sc = nc.dram_tensor("u_sc", [MLP, 64], F32, kind="ExternalInput").ap()
    d_pk = nc.dram_tensor("d_pk", [D, MLP // 2], U16, kind="ExternalInput").ap()
    d_sc = nc.dram_tensor("d_sc", [D, MLP // 32], F32, kind="ExternalInput").ap()
    out = nc.dram_tensor("out", [TQ, D], F32, kind="ExternalOutput").ap()

    tap_aps = {}

    def tap(name, shape, dtype):
        if name in taps:
            tap_aps[name] = nc.dram_tensor("tap_" + name, shape, dtype,
                                           kind="ExternalOutput").ap()
            return tap_aps[name]
        return None

    t_xt = tap("xt", [128, DC * TK], BF16)
    t_rop0 = tap("rop0", [128, 2048], F32)
    t_ktb = tap("ktb", [128, 4096], F32)
    t_dbg = tap("dbg", [128, 1024], F32)
    t_kt = tap("kt", [128, KH * TK], BF16)
    t_v = tap("v", [128, NSC * KH * HD], BF16)
    t_qt = tap("qt", [128, NH * TQ], BF16)
    t_enc = tap("enc", [128, NH * TQ], BF16)
    t_y = tap("y", [TQ, D], F32)
    t_int = tap("int", [128, GSETS * TQ], BF16)
    t_e2 = tap("e2", [TQ, D], F32)

    es = ExitStack()
    with es:
        def sb(name, shape, dt):
            return es.enter_context(nc.sbuf_tensor(name, shape, dt))

        def psmt(name, shape):
            return es.enter_context(nc.psum_tensor(name, shape, F32))

        scr = sb("scr", [128, 1024], F32)
        qtab = sb("qtab", [128, 8, 4, 64], F32)
        stat = sb("stat", [128, 16], F32)
        prjw = sb("prj", [128, 1024], F32)
        prj = prjw[:, 0:512]
        rop = sb("rop", [128, 512], BF16)
        rbc = sb("rbc", [128, 512], F32)
        valv = sb("valv", [128, NSC], F32)
        koff = sb("koff", [128, 1], F32)
        invts = sb("invts", [128, 64], F32)
        piota = sb("piota", [128, 1], I32)
        piof = sb("piof", [128, 1], F32)
        onesb = sb("onesb", [128, 1], BF16)
        onesf1 = sb("onesf1", [1, 128], F32)
        epsln = sb("epsln", [128, 1], F32)
        epsrm = sb("epsrm", [128, 1], F32)
        qksc = sb("qksc", [1, 256], F32)

        AR = sb("AR", [128, 92 * K1], U16)

        def AV(o_k1, n_u16, dt, shape=None):
            ap = AR[:, o_k1 * K1:o_k1 * K1 + n_u16].bitcast(dt)
            if shape is not None:
                lead = "p (" + " ".join(f"a{i}" for i in range(len(shape))) + ") -> p " + \
                   " ".join(f"a{i}" for i in range(len(shape)))
                kw = {f"a{i}": s for i, s in enumerate(shape)}
                ap = ap.rearrange(lead, **kw)
            return ap

        kT = AV(0, 16 * K1, BF16, (KH, TK))
        masks = AV(32, 4 * K1, BF16, (8, 512))
        v_arr = AV(16, 16 * K1, BF16, (NSC, KH, HD))
        e_t = [AV(4 * tt, 4 * K1, F32, (D,)) for tt in range(8)]
        xT_q = AV(32, 16 * K1, BF16, (DC, TQ))
        wo_grp = [AV(32 + 8 * i, 8 * K1, BF16, (NH, 512)) for i in range(2)]
        planes = [AV(32 + 4 * i, 4 * K1, BF16, (4, 1024)) for i in range(2)]
        WT = [AV(40 + 4 * i, 4 * K1, BF16, (32, 128)) for i in range(2)]
        xT_k = AV(48, 16 * K1, BF16, (DC, TQ))
        qT = AV(48, 16 * K1, BF16, (NH, TQ))
        inter = AV(48, 8 * K1, BF16, (GSETS, TQ))
        WdT = [AV(56, 4 * K1, BF16, (8, 512)), AV(60, 4 * K1, BF16, (8, 512))]
        ropet = AV(64, 4 * K1, F32, (32, 64))
        wgrp = [AV(64 + 8 * i, 8 * K1, BF16, (DC, 512)) for i in range(2)]
        enc = AV(64, 16 * K1, BF16, (NH, TQ))
        yTe = AV(64, 8 * K1, BF16, (8, TQ))
        yTo = AV(72, 8 * K1, BF16, (8, TQ))
        xf32 = [AV(80 + 4 * i, 4 * K1, F32, (D,)) for i in range(2)]
        ktab = AV(80, 8 * K1, F32, (16, 4, 64))
        ptil = AV(80, 6 * K1, BF16, (12, 512))
        lnscr = AV(80, 4 * K1, F32, (D,))
        gupk = [AV(80 + 2 * i, 2 * K1, U16, (2, 1024)) for i in range(4)]
        xbf = [AV(88 + 2 * i, 2 * K1, BF16, (D,)) for i in range(2)]
        yev = AV(88, 2 * K1, BF16, (D,))
        wstg = AV(88, 2 * K1, F32, (1024,))
        dnp2 = qtab[:].rearrange("p a b c -> p (a b c)")[:, 1536:2048].bitcast(
            U16).rearrange("p (a b) -> p a b", a=2)
        dnpk = [AV(88 + i, 1 * K1, U16, (2, 512)) for i in range(2)] + [dnp2]
        dplanes = AV(90, 2 * K1, BF16, (2, 1024))

        dscT = scr[:].bitcast(U16)
        dscS = prjw[:].bitcast(U16)
        plv_bf = [AV(32 + 4 * i, 4 * K1, BF16, (2, 2, 1024)) for i in range(2)]
        plv_u16 = [AV(32 + 4 * i, 4 * K1, U16, (2, 2, 1024)) for i in range(2)]
        dpl_bf = AV(90, 2 * K1, BF16, (2, 2, 512))
        dpl_u16 = AV(90, 2 * K1, U16, (2, 2, 512))
        glscr = rop[:]
        gscf = sb("gscf", [128, 8, 64], F32)
        dscf = sb("dscf", [128, 6, 32], F32)

        pp = [psmt(f"pp{i}", [128, 512]) for i in range(6)]
        pp_r = psmt("pp_r", [128, 512])
        pp6 = psmt("pp6", [128, 512])

        SEM, C = {}, {}
        for s in ("dC", "dX0", "dX1", "dW0", "dW1", "dPK0", "dPK1", "dDN0",
                  "dDN1", "dT0", "dT1", "dU0", "dU1", "dXB", "dST", "v", "a",
                  "t", "p"):
            SEM[s] = es.enter_context(nc.semaphore(s))
            C[s] = 0

        def inc(instr, s, k=1):
            instr.then_inc(SEM[s], k)
            C[s] += k

        def wt(eng, s, v=None):
            eng.wait_ge(SEM[s], C[s] if v is None else v)

        DSEMS = ("dC", "dX0", "dX1", "dW0", "dW1", "dPK0", "dPK1", "dDN0",
                 "dDN1", "dT0", "dT1", "dU0", "dU1", "dXB", "dST")
        _waited = {sname: 0 for sname in DSEMS}
        _rstate = {"xp": False}

        def ring_sync(sy):
            for sname in DSEMS:
                if C[sname] > _waited[sname]:
                    sy.wait_ge(SEM[sname], C[sname])
                    _waited[sname] = C[sname]

        def VC(v_, instr):
            inc(instr, "v")
            wt(v_, "v")

        def ring_dma(sy, *a, transpose=False, **kw):
            if transpose or _rstate["xp"]:
                ring_sync(sy)
            _rstate["xp"] = transpose
            return sy.dma_start(*a, transpose=transpose, **kw)

        block = es.enter_context(nc.Block())

        @block.sync
        def _(sy):
            for d1 in (
                ring_dma(sy, out=koff[:], in_=koff_in.rearrange("(p o) -> p o", o=1)),
                ring_dma(sy, out=invts[:], in_=invts_in),
                ring_dma(sy, out=valv[:], in_=val_in.rearrange("(c p) -> p c", p=128),
                             allow_slow_non_contiguous=True),
                ring_dma(sy, out=qksc[:, 0:128], in_=qns),
                ring_dma(sy, out=qksc[:, 128:256], in_=kns),
            ):
                inc(d1, "dC", 16)

        @block.gpsimd
        def _(g_):
            g_.iota(piota[:], pattern=[[0, 1]], base=0, channel_multiplier=1)
            inc(g_.memset(onesb[:], 1.0), "p")

        @block.vector
        def _(v_):
            wt(v_, "dC")
            wt(v_, "p")
            VC(v_, v_.tensor_copy(piof[:], piota[:]))
            v_.memset(onesf1[:], 1.0)
            v_.memset(epsln[:], 1e-5)
            VC(v_, v_.memset(epsrm[:], 1e-6))
            arg = scr[:, 0:64]
            n1 = scr[:, 64:128]
            ni = scr[:, 128:192].bitcast(I32)[:, 0:64]
            posf = scr[:, 192:193]
            for c in range(16):
                VC(v_, v_.tensor_scalar(posf, piota[:], koff[:, 0:1], float(128 * c),
                                        ALU.add, ALU.add))
                VC(v_, v_.tensor_scalar(arg, invts[:], posf, None, ALU.mult))
                for j, shift in ((0, 0.0), (1, HALF_PI)):
                    dst = ropet[:, 2 * c + j, :]
                    VC(v_, v_.tensor_scalar(n1, arg, shift, 1.0 / TWO_PI, ALU.add, ALU.mult))
                    VC(v_, v_.tensor_copy(ni, n1))
                    VC(v_, v_.tensor_scalar(n1, arg, shift, None, ALU.add))
                    VC(v_, v_.scalar_tensor_tensor(dst, ni, -TWO_PI, n1, ALU.mult, ALU.add))
                    VC(v_, v_.tensor_scalar(n1, dst, float(np.pi), None, ALU.is_gt))
                    VC(v_, v_.scalar_tensor_tensor(dst, n1, -TWO_PI, dst, ALU.mult, ALU.add))
                    VC(v_, v_.tensor_scalar(n1, dst, float(-np.pi), None, ALU.is_lt))
                    VC(v_, v_.scalar_tensor_tensor(dst, n1, TWO_PI, dst, ALU.mult, ALU.add))
            iv = v_.tensor_copy(scr[:, 256:257], piof[:])
            inc(iv, "v")
        v_args = C["v"]

        @block.scalar
        def _(s_):
            wt(s_, "v", v_args)
            a1 = s_.activation(ropet[:].rearrange("p a b -> p (a b)"),
                               ropet[:].rearrange("p a b -> p (a b)"), AF.Sin)
            inc(a1, "a")
        a_sin = C["a"]

        @block.tensor
        def _(t_):
            wt(t_, "a", a_sin)
            t_.matmul(pp[0][:, 0:128], onesf1[:], qksc[0:1, 0:128], start=True, stop=True)
            tm = t_.matmul(pp[0][:, 128:256], onesf1[:], qksc[0:1, 128:256],
                           start=True, stop=True)
            inc(tm, "t")
        t_scb = C["t"]

        @block.vector
        def _(v_):
            wt(v_, "t", t_scb)
            qsR = scr[:, 0:128]
            ksR = scr[:, 128:256]
            VC(v_, v_.tensor_copy(qsR, pp[0][:, 0:128]))
            VC(v_, v_.tensor_copy(ksR, pp[0][:, 128:256]))
            for c in range(8, 16):
                sinc, cosc = ropet[:, 2 * c, :], ropet[:, 2 * c + 1, :]
                v_.tensor_tensor(qtab[:, c - 8, 0, :], cosc, qsR[:, 0:64], ALU.mult)
                v_.tensor_tensor(qtab[:, c - 8, 1, :], cosc, qsR[:, 64:128], ALU.mult)
                v_.tensor_tensor(qtab[:, c - 8, 2, :], sinc, qsR[:, 0:64], ALU.mult)
                kv4 = v_.tensor_tensor(qtab[:, c - 8, 3, :], sinc, qsR[:, 64:128],
                                       ALU.mult)
            VC(v_, kv4)
            iv2 = v_.tensor_copy(scr[:, 256:257], piof[:])
            inc(iv2, "v")
        v_tabs = C["v"]

        for sc_i in range(NSC):
            sl = sc_i % 2

            @block.sync
            def _(sy, sc_i=sc_i, sl=sl):
                if sc_i >= 2:
                    wt(sy, "v", C[f"_xc{sc_i - 2}"])
                d1 = ring_dma(sy, out=xf32[sl][:],
                                  in_=x_kv[128 * sc_i:128 * (sc_i + 1), :])
                inc(d1, "dX%d" % sl, 16)

            @block.vector
            def _(v_, sc_i=sc_i, sl=sl):
                wt(v_, "dX%d" % sl, 16 * (sc_i // 2 + 1))
                if sc_i == 0:
                    wt(v_, "v", v_tabs)
                if sc_i >= 2:
                    wt(v_, "dXB", 16 * (sc_i - 1))
                cv = v_.tensor_copy(xbf[sl][:], xf32[sl][:])
                inc(cv, "v")
                C[f"_xc{sc_i}"] = C["v"]

            @block.sync
            def _(sy, sc_i=sc_i, sl=sl):
                wt(sy, "v", C[f"_xc{sc_i}"])
                dst = xT_k if sc_i < 8 else xT_q
                so = sc_i if sc_i < 8 else sc_i - 8
                d1 = ring_dma(sy, out=dst[:, :, 128 * so:128 * (so + 1)],
                                  in_=xbf[sl][:], transpose=True)
                inc(d1, "dXB", 16)
        dXB_xt = C["dXB"]
        C["_p1_v"] = C["v"]

        @block.vector
        def _(v_):
            wt(v_, "dXB", dXB_xt)
            ksR = scr[:, 128:256]
            for c in range(16):
                sinc, cosc = ropet[:, 2 * c, :], ropet[:, 2 * c + 1, :]
                v_.tensor_tensor(ktab[:, c, 0, :], cosc, ksR[:, 0:64], ALU.mult)
                v_.tensor_tensor(ktab[:, c, 1, :], cosc, ksR[:, 64:128], ALU.mult)
                v_.tensor_tensor(ktab[:, c, 2, :], sinc, ksR[:, 0:64], ALU.mult)
                kv4 = v_.tensor_tensor(ktab[:, c, 3, :], sinc, ksR[:, 64:128], ALU.mult)
            VC(v_, kv4)
            iv3 = v_.tensor_copy(scr[:, 256:257], piof[:])
            inc(iv3, "v")
        v_ktab = C["v"]

        if t_rop0 is not None:
            @block.sync
            def _(sy):
                wt(sy, "v", v_ktab)
                inc(ring_dma(sy, out=t_rop0, in_=ropet[:].rearrange("p a b -> p (a b)")), "dST", 16)
                inc(ring_dma(sy, out=t_ktb, in_=ktab[:].rearrange("p a b c -> p (a b c)")), "dST", 16)

        if t_xt is not None:
            @block.sync
            def _(sy):
                wt(sy, "dXB", dXB_xt)
                d0 = ring_dma(sy, out=t_xt.rearrange("p (d t) -> p d t", d=DC)[:, :, 0:1024],
                                  in_=xT_k[:])
                inc(d0, "dST", 16)
                d1 = ring_dma(sy, 
                    out=t_xt.rearrange("p (d t) -> p d t", d=DC)[:, :, 1024:2048],
                    in_=xT_q[:])
                inc(d1, "dST", 16)

        def load_wgrp(dst, col_aps, vkey, extra_ring_wait=None, cast_wait=None):
            nb = len(col_aps)
            for i_ in range(nb):
                wsl = i_ % 2

                @block.sync
                def _(sy, i_=i_, wsl=wsl):
                    if extra_ring_wait is not None and i_ == 0:
                        for sem_name, valw in extra_ring_wait:
                            sy.wait_ge(SEM[sem_name], valw)
                    if i_ >= 2:
                        wt(sy, "v", C[f"_wc_{vkey}_{i_ - 2}"])
                    dst_ap = wstg[:, 512 * wsl:512 * (wsl + 1)]
                    if col_aps[i_].ndim == 3:
                        dst_ap = dst_ap.rearrange("p (a b) -> p a b",
                                                  b=col_aps[i_].shape[2])
                    d1 = ring_dma(sy, out=dst_ap, in_=col_aps[i_])
                    inc(d1, "dW%d" % wsl, 16)

                @block.vector
                def _(v_, i_=i_, wsl=wsl):
                    wt(v_, "dW%d" % wsl, C["dW%d" % wsl])
                    if cast_wait is not None and i_ == 0:
                        v_.wait_ge(SEM[cast_wait[0]], cast_wait[1])
                    cv = v_.tensor_copy(dst(i_), wstg[:, 512 * wsl:512 * (wsl + 1)])
                    inc(cv, "v")
                    C[f"_wc_{vkey}_{i_}"] = C["v"]
            C[f"_wg_{vkey}"] = C["v"]

        def kv_cols(c, gidx):
            cc, gg = gidx // 2, gidx % 2
            return kvk[cc, :, :, :].rearrange("k d h -> d k h")[
                128 * c:128 * (c + 1), 4 * gg:4 * (gg + 1), :]

        def q_cols(c, g):
            return qkw.rearrange("n d h -> d n h")[128 * c:128 * (c + 1),
                                                   4 * g:4 * (g + 1), :]

        kv_order = [2, 3, 0, 1]
        load_wgrp(lambda i: wgrp[0][:, i, :], [kv_cols(c, 2) for c in range(DC)],
                  "kv0", extra_ring_wait=[("v", v_ktab)])
        for gi in range(4):
            gidx = kv_order[gi]
            gslot = gi % 2
            if gi + 1 < 4:
                load_wgrp(lambda i, s=1 - gslot: wgrp[s][:, i, :],
                          [kv_cols(c, kv_order[gi + 1]) for c in range(DC)],
                          f"kv{gi + 1}",
                          extra_ring_wait=[("v", C[f"_wg_kv{gi}"])],
                          cast_wait=(("t", C[f"_kvp_{gi - 1}_15"]) if gi >= 1
                                     else None))
            wgready = C[f"_wg_kv{gi}"]
            for st in range(NSC):
                ppj = pp[st % 2]
                tmark = f"_kvp_{gi}_{st}"

                @block.tensor
                def _(t_, st=st, ppj=ppj, gslot=gslot, wgready=wgready, gi=gi, tmark=tmark):
                    wt(t_, "v", wgready)
                    if gi == 0 and st < 2:
                        wt(t_, "dXB", dXB_xt)
                    prev = C.get(f"_kve_{gi}_{st - 2}" if st >= 2 else
                                 f"_kve_{gi - 1}_{NSC - 2 + st}", 0)
                    if prev:
                        wt(t_, "v", prev)
                    xsrc = xT_k if st < 8 else xT_q
                    so = st if st < 8 else st - 8
                    for dc in range(DC):
                        mm = t_.matmul(ppj[:], xsrc[:, dc, 128 * so:128 * (so + 1)],
                                       wgrp[gslot][:, dc, :],
                                       start=(dc == 0), stop=(dc == DC - 1))
                    inc(mm, "t")
                    C[tmark] = C["t"]

                if gidx >= 2:
                    gg = gidx - 2

                    @block.vector
                    def _(v_, st=st, ppj=ppj, gg=gg, gi=gi, tmark=tmark):
                        wt(v_, "t", C[tmark])
                        cv = v_.tensor_copy(
                            v_arr[:, st, 4 * gg:4 * gg + 4, :].rearrange(
                                "p a b -> p (a b)"), ppj[:])
                        inc(cv, "v")
                        C[f"_kve_{gi}_{st}"] = C["v"]
                else:
                    gg = gidx

                    @block.vector
                    def _(v_, st=st, ppj=ppj, gg=gg, gi=gi, tmark=tmark):
                        wt(v_, "t", C[tmark])
                        cv = v_.tensor_copy(prj[:], ppj[:])
                        inc(cv, "v")
                        C[f"_kev_{gi}_{st}"] = C["v"]

                    @block.scalar
                    def _(s_, gi=gi, st=st):
                        wt(s_, "v", C[f"_kev_{gi}_{st}"])
                        a1 = s_.activation(rbc[:], prj[:], AF.Square)
                        inc(a1, "a")
                        C[f"_ksa_{gi}_{st}"] = C["a"]

                    @block.vector
                    def _(v_, gi=gi, st=st):
                        wt(v_, "a", C[f"_ksa_{gi}_{st}"])
                        rv = v_.tensor_reduce(stat[:, 0:4],
                                              rbc[:].rearrange("p (h x) -> p h x", x=128),
                                              axis=AX.X, op=ALU.add)
                        inc(rv, "v")
                        C[f"_krms_{gi}_{st}"] = C["v"]

                    @block.scalar
                    def _(s_, gi=gi, st=st):
                        wt(s_, "v", C[f"_krms_{gi}_{st}"])
                        a1 = s_.activation(stat[:, 4:8], stat[:, 0:4], AF.Sqrt,
                                           scale=1.0 / 128.0, bias=epsrm[:, 0:1])
                        inc(a1, "a")
                        C[f"_ksq_{gi}_{st}"] = C["a"]

                    @block.vector
                    def _(v_, st=st, gg=gg, gi=gi, ppj=ppj):
                        wt(v_, "a", C[f"_ksq_{gi}_{st}"])
                        wt(v_, "dXB")
                        VC(v_, v_.reciprocal(stat[:, 0:4], stat[:, 4:8]))
                        for hh in range(4):
                            iap = v_.tensor_scalar(prj[:, 128 * hh:128 * (hh + 1)],
                                                   prj[:, 128 * hh:128 * (hh + 1)],
                                                   stat[:, hh:hh + 1], None, ALU.mult)
                            if hh == 3:
                                VC(v_, iap)
                        x1 = prj[:].rearrange("p (h x) -> p h x", x=128)[:, :, 0:64]
                        x2 = prj[:].rearrange("p (h x) -> p h x", x=128)[:, :, 64:128]
                        o1 = rop[:].rearrange("p (h x) -> p h x", x=128)[:, :, 0:64]
                        o2 = rop[:].rearrange("p (h x) -> p h x", x=128)[:, :, 64:128]
                        tA = scr[:, 512:768].rearrange("p (h x) -> p h x", x=64)
                        tB = scr[:, 768:1024].rearrange("p (h x) -> p h x", x=64)

                        def tb(j):
                            return ktab[:, st, j, :].unsqueeze(1).broadcast_to((128, 4, 64))
                        VC(v_, v_.tensor_tensor(tA, x1, tb(0), ALU.mult))
                        VC(v_, v_.tensor_tensor(tB, x2, tb(3), ALU.mult))
                        VC(v_, v_.tensor_tensor(o1, tA, tB, ALU.subtract))
                        VC(v_, v_.tensor_tensor(tA, x2, tb(1), ALU.mult))
                        VC(v_, v_.tensor_tensor(tB, x1, tb(2), ALU.mult))
                        cv = v_.tensor_tensor(o2, tA, tB, ALU.add)
                        inc(cv, "v")
                        C[f"_kve_{gi}_{st}"] = C["v"]

                    @block.sync
                    def _(sy, st=st, gg=gg, gi=gi):
                        wt(sy, "v", C[f"_kve_{gi}_{st}"])
                        d1 = ring_dma(sy, 
                            out=kT[:, 4 * gg:4 * gg + 4, 128 * st:128 * (st + 1)],
                            in_=rop[:], transpose=True)
                        inc(d1, "dXB", 16)
        C["_kv_v"] = C["v"]
        C["_kv_t"] = C["t"]
        dXB_k = C["dXB"]

        if t_dbg is not None:
            @block.sync
            def _(sy):
                wt(sy, "v", C["_kv_v"])
                inc(ring_dma(sy, out=t_dbg[:, 0:16], in_=stat[:]), "dST", 16)
                inc(ring_dma(sy, out=t_dbg[:, 16:528], in_=prj[:]), "dST", 16)
                inc(ring_dma(sy, out=t_dbg[:, 528:1040] if False else t_dbg[:, 512:1024], in_=rbc[:]), "dST", 16)

        if t_kt is not None:
            @block.sync
            def _(sy):
                wt(sy, "dXB", dXB_k)
                inc(ring_dma(sy, out=t_kt, in_=kT[:].rearrange("p a b -> p (a b)")),
                    "dST", 16)
        if t_v is not None:
            @block.sync
            def _(sy):
                wt(sy, "v", C["_kv_v"])
                inc(ring_dma(sy, out=t_v,
                                 in_=v_arr[:].rearrange("p a b c -> p (a b c)")),
                    "dST", 16)

        if stage >= 3:
            load_wgrp(lambda i: wgrp[0][:, i, :], [q_cols(c, 0) for c in range(DC)],
                      "q0", extra_ring_wait=[("v", C["_wg_kv3"])],
                      cast_wait=("t", C["_kvp_2_15"]))
            for gq in range(4):
                gslot = gq % 2
                if gq + 1 < 4:
                    load_wgrp(lambda i, s=1 - gslot: wgrp[s][:, i, :],
                              [q_cols(c, gq + 1) for c in range(DC)], f"q{gq + 1}",
                              extra_ring_wait=[("v", C[f"_wg_q{gq}"])],
                              cast_wait=(("t", C[f"_qp_{gq - 1}_7"]) if gq >= 1
                                         else ("t", C["_kvp_3_15"])))
                wgready = C[f"_wg_q{gq}"]
                for tt in range(NTT):
                    ppj = pp[tt % 2]
                    tmark = f"_qp_{gq}_{tt}"

                    @block.tensor
                    def _(t_, tt=tt, ppj=ppj, gslot=gslot, wgready=wgready, gq=gq, tmark=tmark):
                        wt(t_, "v", wgready)
                        if gq == 0 and tt < 2:
                            wt(t_, "v", C["_kv_v"])
                        prev = C.get(f"_qe_{gq}_{tt - 2}" if tt >= 2 else
                                     f"_qe_{gq - 1}_{NTT - 2 + tt}", 0)
                        if prev:
                            wt(t_, "v", prev)
                        for dc in range(DC):
                            mm = t_.matmul(ppj[:], xT_q[:, dc, 128 * tt:128 * (tt + 1)],
                                           wgrp[gslot][:, dc, :],
                                           start=(dc == 0), stop=(dc == DC - 1))
                        inc(mm, "t")
                        C[tmark] = C["t"]

                    @block.vector
                    def _(v_, tt=tt, ppj=ppj, gq=gq, tmark=tmark):
                        wt(v_, "t", C[tmark])
                        cv = v_.tensor_copy(prj[:], ppj[:])
                        inc(cv, "v")
                        C[f"_qev_{gq}_{tt}"] = C["v"]

                    @block.scalar
                    def _(s_, gq=gq, tt=tt):
                        wt(s_, "v", C[f"_qev_{gq}_{tt}"])
                        a1 = s_.activation(rbc[:], prj[:], AF.Square)
                        inc(a1, "a")
                        C[f"_qsa_{gq}_{tt}"] = C["a"]

                    @block.vector
                    def _(v_, gq=gq, tt=tt):
                        wt(v_, "a", C[f"_qsa_{gq}_{tt}"])
                        rv = v_.tensor_reduce(stat[:, 0:4],
                                              rbc[:].rearrange("p (h x) -> p h x", x=128),
                                              axis=AX.X, op=ALU.add)
                        inc(rv, "v")
                        C[f"_qrms_{gq}_{tt}"] = C["v"]

                    @block.scalar
                    def _(s_, gq=gq, tt=tt):
                        wt(s_, "v", C[f"_qrms_{gq}_{tt}"])
                        a1 = s_.activation(stat[:, 4:8], stat[:, 0:4], AF.Sqrt,
                                           scale=1.0 / 128.0, bias=epsrm[:, 0:1])
                        inc(a1, "a")
                        C[f"_qsq_{gq}_{tt}"] = C["a"]

                    @block.vector
                    def _(v_, tt=tt, gq=gq):
                        wt(v_, "a", C[f"_qsq_{gq}_{tt}"])
                        wt(v_, "dXB")
                        VC(v_, v_.reciprocal(stat[:, 0:4], stat[:, 4:8]))
                        VC(v_, v_.tensor_scalar(stat[:, 0:4], stat[:, 0:4], ATTN_SCALE,
                                                None, ALU.mult))
                        for hh in range(4):
                            iap = v_.tensor_scalar(prj[:, 128 * hh:128 * (hh + 1)],
                                                   prj[:, 128 * hh:128 * (hh + 1)],
                                                   stat[:, hh:hh + 1], None, ALU.mult)
                            if hh == 3:
                                VC(v_, iap)
                        x1 = prj[:].rearrange("p (h x) -> p h x", x=128)[:, :, 0:64]
                        x2 = prj[:].rearrange("p (h x) -> p h x", x=128)[:, :, 64:128]
                        o1 = rop[:].rearrange("p (h x) -> p h x", x=128)[:, :, 0:64]
                        o2 = rop[:].rearrange("p (h x) -> p h x", x=128)[:, :, 64:128]
                        tA = scr[:, 512:768].rearrange("p (h x) -> p h x", x=64)
                        tB = scr[:, 768:1024].rearrange("p (h x) -> p h x", x=64)

                        def tb(j):
                            return qtab[:, tt, j, :].unsqueeze(1).broadcast_to((128, 4, 64))
                        VC(v_, v_.tensor_tensor(tA, x1, tb(0), ALU.mult))
                        VC(v_, v_.tensor_tensor(tB, x2, tb(3), ALU.mult))
                        VC(v_, v_.tensor_tensor(o1, tA, tB, ALU.subtract))
                        VC(v_, v_.tensor_tensor(tA, x2, tb(1), ALU.mult))
                        VC(v_, v_.tensor_tensor(tB, x1, tb(2), ALU.mult))
                        cv = v_.tensor_tensor(o2, tA, tB, ALU.add)
                        inc(cv, "v")
                        C[f"_qe_{gq}_{tt}"] = C["v"]

                    @block.sync
                    def _(sy, tt=tt, gq=gq):
                        wt(sy, "v", C[f"_qe_{gq}_{tt}"])
                        d1 = ring_dma(sy, 
                            out=qT[:, 4 * gq:4 * gq + 4, 128 * tt:128 * (tt + 1)],
                            in_=rop[:], transpose=True)
                        inc(d1, "dXB", 16)
            C["_q_v"] = C["v"]
            dXB_q = C["dXB"]

            if t_qt is not None:
                @block.sync
                def _(sy):
                    wt(sy, "dXB", dXB_q)
                    inc(ring_dma(sy, out=t_qt, in_=qT[:].rearrange("p a b -> p (a b)")),
                        "dST", 16)

        if stage >= 4:
            @block.gpsimd
            def _(g_):
                g_.wait_ge(SEM["v"], C["_q_v"])
                inc(g_.iota(scr[:, 512:1024].bitcast(I32)[:, 0:512],
                            pattern=[[1, 512]], base=0, channel_multiplier=0), "p")
            p_iota2 = C["p"]

            @block.vector
            def _(v_):
                wt(v_, "v", C["_q_v"])
                wt(v_, "p", p_iota2)
                dmask = scr[:, 0:512]
                VC(v_, v_.tensor_scalar(dmask, scr[:, 512:1024].bitcast(I32)[:, 0:512],
                                        piof[:, 0:1], None, ALU.subtract))
                for k in range(4):
                    v_.tensor_scalar(masks[:, k, :], dmask, float(128 * k),
                                     None, ALU.is_lt)
                for k in range(8, 12):
                    mk = v_.tensor_scalar(masks[:, k - 4, :], dmask,
                                          float(128 * k - 1024), None, ALU.is_ge)
                VC(v_, mk)
                iv4 = v_.tensor_copy(scr[:, 512:513], piof[:])
                inc(iv4, "v")
            v_masks = C["v"]

            for g in range(2):
                for n in range(NH):
                    pe = pp[2 + (n % 2)]
                    smps = pp[4 + (n % 2)]
                    for k in range(12):
                        c = 4 * g + k
                        sp = pp[k % 2]
                        smark = f"_s_{g}_{n}_{k}"

                        @block.tensor
                        def _(t_, g=g, n=n, k=k, c=c, sp=sp, smark=smark):
                            if g == 0 and n == 0 and k == 0:
                                wt(t_, "dXB", C["dXB"])
                                wt(t_, "v", v_masks)
                            prev = C.get(f"_ex_{g}_{n}_{k - 2}" if k >= 2 else
                                         (f"_ex_{g}_{n - 1}_{10 + k}" if n >= 1 else
                                          (f"_ex_{g - 1}_{NH - 1}_{10 + k}" if g >= 1
                                           else "_none")), 0)
                            if prev:
                                wt(t_, "a", prev)
                            mm = t_.matmul(sp[:], kT[:, n // 2, 128 * c:128 * (c + 1)],
                                           qT[:, n, 512 * g:512 * (g + 1)],
                                           start=True, stop=True)
                            inc(mm, "t")
                            C[smark] = C["t"]

                        @block.scalar
                        def _(s_, g=g, n=n, k=k, c=c, sp=sp, smark=smark):
                            wt(s_, "t", C[smark])
                            if k == 0:
                                prevpv = C.get(f"_pv_{g}_{n - 1}" if n >= 1 else
                                               (f"_pv_{g - 1}_{NH - 1}" if g >= 1
                                                else "_none"), 0)
                                if prevpv:
                                    wt(s_, "t", prevpv)
                            a1 = s_.activation(ptil[:, k, :], sp[:], AF.Exp,
                                               bias=valv[:, c:c + 1], scale=1.0)
                            inc(a1, "a")
                            C[f"_ex_{g}_{n}_{k}"] = C["a"]

                    @block.vector
                    def _(v_, g=g, n=n):
                        wt(v_, "a", C[f"_ex_{g}_{n}_11"])
                        for k in (0, 1, 2, 3):
                            v_.tensor_tensor(ptil[:, k, :], ptil[:, k, :],
                                             masks[:, k, :], ALU.mult)
                        for k in (8, 9, 10, 11):
                            mv = v_.tensor_tensor(ptil[:, k, :], ptil[:, k, :],
                                                  masks[:, k - 4, :], ALU.mult)
                        inc(mv, "v")
                        C[f"_msk_{g}_{n}"] = C["v"]

                    @block.tensor
                    def _(t_, g=g, n=n, pe=pe, smps=smps):
                        wt(t_, "v", C[f"_msk_{g}_{n}"])
                        prev = C.get(f"_enc_{g}_{n - 2}" if n >= 2 else
                                     (f"_enc_{g - 1}_{NH - 2 + n}" if g >= 1
                                      else "_none"), 0)
                        if prev:
                            wt(t_, "v", prev)
                        for k in range(12):
                            c = 4 * g + k
                            t_.matmul(pe[:], v_arr[:, c, n // 2, :], ptil[:, k, :],
                                      start=(k == 0), stop=(k == 11))
                        for k in range(12):
                            mm = t_.matmul(smps[0:1, :], onesb[:], ptil[:, k, :],
                                           start=(k == 0), stop=(k == 11))
                        inc(mm, "t")
                        C[f"_pv_{g}_{n}"] = C["t"]

                    @block.vector
                    def _(v_, g=g, n=n, smps=smps):
                        wt(v_, "t", C[f"_pv_{g}_{n}"])
                        VC(v_, v_.tensor_copy(scr[0:1, 0:512], smps[0:1, :]))
                        rv = v_.reciprocal(scr[0:1, 512:1024], scr[0:1, 0:512])
                        inc(rv, "v")
                        C[f"_rc_{g}_{n}"] = C["v"]

                    @block.tensor
                    def _(t_, g=g, n=n):
                        wt(t_, "v", C[f"_rc_{g}_{n}"])
                        prev = C.get(f"_enc_{g}_{n - 1}" if n >= 1 else
                                     (f"_enc_{g - 1}_{NH - 1}" if g >= 1 else "_none"), 0)
                        if prev:
                            wt(t_, "v", prev)
                        mm = t_.matmul(pp_r[:], onesf1[:], scr[0:1, 512:1024],
                                       start=True, stop=True)
                        inc(mm, "t")
                        C[f"_bc_{g}_{n}"] = C["t"]

                    @block.vector
                    def _(v_, g=g, n=n, pe=pe):
                        wt(v_, "t", C[f"_bc_{g}_{n}"])
                        VC(v_, v_.tensor_copy(rbc[:], pp_r[:]))
                        cv = v_.tensor_tensor(enc[:, n, 512 * g:512 * (g + 1)],
                                              pe[:], rbc[:], ALU.mult)
                        inc(cv, "v")
                        C[f"_enc_{g}_{n}"] = C["v"]
            C["_attn_v"] = C["v"]
            C["_attn_t"] = C["t"]

            if t_enc is not None:
                @block.sync
                def _(sy):
                    wt(sy, "v", C["_attn_v"])
                    inc(ring_dma(sy, out=t_enc,
                                     in_=enc[:].rearrange("p a b -> p (a b)")),
                        "dST", 16)

        if stage >= 5:
            for tt in range(NTT):
                @block.sync
                def _(sy, tt=tt):
                    if tt == 0:
                        wt(sy, "t", C["_attn_t"])
                    d1 = ring_dma(sy, out=e_t[tt][:],
                                      in_=x_kv[TQ + 128 * tt:TQ + 128 * (tt + 1), :])
                    inc(d1, "dC", 16)
            dC_x = C["dC"]

            for dg in range(4):
                gslot = dg % 2
                for n in range(NH):
                    wsl = n % 2

                    @block.sync
                    def _(sy, dg=dg, n=n, wsl=wsl):
                        if dg == 0 and n == 0:
                            wt(sy, "t", C["_attn_t"])
                        if n >= 2 or dg >= 1:
                            wt(sy, "v", C[f"_oc_{dg}_{n - 2}" if n >= 2 else
                                          f"_oc_{dg - 1}_{NH - 2 + n}"])
                        d1 = ring_dma(sy, out=wstg[:, 512 * wsl:512 * (wsl + 1)],
                                          in_=okw[n, :, 512 * dg:512 * (dg + 1)])
                        inc(d1, "dW%d" % wsl, 16)

                    @block.vector
                    def _(v_, dg=dg, n=n, wsl=wsl, gslot=gslot):
                        wt(v_, "dW%d" % wsl, C["dW%d" % wsl])
                        if dg == 0 and n == 0:
                            wt(v_, "t", C["_attn_t"])
                        if dg >= 2 and n == 0:
                            wt(v_, "t", C[f"_op_{dg - 2}_7"])
                        cv = v_.tensor_copy(wo_grp[gslot][:, n, :],
                                            wstg[:, 512 * wsl:512 * (wsl + 1)])
                        inc(cv, "v")
                        C[f"_oc_{dg}_{n}"] = C["v"]
                wgready = C[f"_oc_{dg}_{NH - 1}"]
                for tt in range(NTT):
                    ppj = pp[tt % 2]
                    tmark = f"_op_{dg}_{tt}"

                    @block.tensor
                    def _(t_, dg=dg, tt=tt, ppj=ppj, gslot=gslot, wgready=wgready, tmark=tmark):
                        wt(t_, "v", wgready)
                        prev = C.get(f"_oe_{dg}_{tt - 2}" if tt >= 2 else
                                     (f"_oe_{dg - 1}_{NTT - 2 + tt}" if dg >= 1
                                      else "_none"), 0)
                        if prev:
                            wt(t_, "v", prev)
                        for n in range(NH):
                            mm = t_.matmul(ppj[:], enc[:, n, 128 * tt:128 * (tt + 1)],
                                           wo_grp[gslot][:, n, :],
                                           start=(n == 0), stop=(n == NH - 1))
                        inc(mm, "t")
                        C[tmark] = C["t"]

                    @block.vector
                    def _(v_, dg=dg, tt=tt, ppj=ppj, tmark=tmark):
                        wt(v_, "t", C[tmark])
                        if dg == 0 and tt == 0:
                            wt(v_, "dC", dC_x)
                        cv = v_.tensor_tensor(e_t[tt][:, 512 * dg:512 * (dg + 1)],
                                              e_t[tt][:, 512 * dg:512 * (dg + 1)],
                                              ppj[:], ALU.add)
                        inc(cv, "v")
                        C[f"_oe_{dg}_{tt}"] = C["v"]
            C["_op_t"] = C["t"]

            for tt in range(NTT):
                @block.scalar
                def _(s_, tt=tt):
                    s_.wait_ge(SEM["v"], C[f"_oe_3_{tt}"])
                    if tt >= 1:
                        s_.wait_ge(SEM["v"], C[f"_l1s_{tt - 1}"])
                    a1 = s_.activation(lnscr[:], e_t[tt][:], AF.Square)
                    inc(a1, "a")
                    C[f"_l1q_{tt}"] = C["a"]

                @block.vector
                def _(v_, tt=tt):
                    wt(v_, "v", C[f"_oe_3_{tt}"])
                    wt(v_, "a", C[f"_l1q_{tt}"])
                    et = e_t[tt][:]
                    VC(v_, v_.tensor_reduce(stat[:, 0:1], et, axis=AX.X, op=ALU.add))
                    VC(v_, v_.tensor_reduce(stat[:, 1:2], lnscr[:], axis=AX.X, op=ALU.add))
                    VC(v_, v_.tensor_scalar(stat[:, 0:1], stat[:, 0:1], 1.0 / D, None, ALU.mult))
                    VC(v_, v_.tensor_scalar(stat[:, 1:2], stat[:, 1:2], 1.0 / D, None, ALU.mult))
                    VC(v_, v_.tensor_scalar(stat[:, 2:3], stat[:, 0:1], stat[:, 0:1],
                                            None, ALU.mult))
                    cv = v_.tensor_tensor(stat[:, 1:2], stat[:, 1:2], stat[:, 2:3],
                                          ALU.subtract)
                    inc(cv, "v")
                    C[f"_l1s_{tt}"] = C["v"]

                @block.scalar
                def _(s_, tt=tt):
                    wt(s_, "v", C[f"_l1s_{tt}"])
                    aln = s_.activation(stat[:, 2:3], stat[:, 1:2], AF.Ln, bias=epsln[:, 0:1])
                    inc(aln, "a")
                    wt(s_, "a")
                    a1 = s_.activation(stat[:, 3:4], stat[:, 2:3], AF.Exp, scale=-0.5)
                    inc(a1, "a")
                    C[f"_l1r_{tt}"] = C["a"]

                @block.vector
                def _(v_, tt=tt):
                    wt(v_, "a", C[f"_l1r_{tt}"])
                    et = e_t[tt][:]
                    v_.tensor_scalar(et, et, stat[:, 0:1], stat[:, 3:4],
                                     ALU.subtract, ALU.mult)
                    ev = et.rearrange("p (a two) -> p two a", two=2)
                    if tt >= 1:
                        wt(v_, "dXB", 16 * 0 + C["dXB"])
                    v_.tensor_copy(yev[:, 0:1024], ev[:, 0, :])
                    cv = v_.tensor_copy(yev[:, 1024:2048], ev[:, 1, :])
                    inc(cv, "v")
                    C[f"_ysp_{tt}"] = C["v"]

                @block.sync
                def _(sy, tt=tt):
                    wt(sy, "v", C[f"_ysp_{tt}"])
                    if tt == 0:
                        wt(sy, "t", C["_op_t"])
                    d0 = ring_dma(sy, out=yTe[:, :, 128 * tt:128 * (tt + 1)],
                                      in_=yev[:, 0:1024], transpose=True)
                    inc(d0, "dXB", 16)
                    d1 = ring_dma(sy, out=yTo[:, :, 128 * tt:128 * (tt + 1)],
                                      in_=yev[:, 1024:2048], transpose=True)
                    inc(d1, "dXB", 16)
            C["_y_xb"] = C["dXB"]
            C["_ln1_v"] = C["v"]

            if t_y is not None:
                for tt in range(NTT):
                    @block.sync
                    def _(sy, tt=tt):
                        wt(sy, "v", C["_ln1_v"])
                        inc(ring_dma(sy, out=t_y[128 * tt:128 * (tt + 1), :],
                                         in_=e_t[tt][:]), "dST", 16)

        if stage >= 6:
            LUT_SPEC = [
                (7, 6, 12),
                (0x70, 2, 8),
            ]

            def gu_dequant(v_, s_):
                sl4 = s_ % 4
                psl = s_ % 2
                wt(v_, "dPK%d" % (s_ % 2), 64 * (s_ // 2 + 1))
                if s_ >= 2:
                    wt(v_, "dT%d" % psl, 16 * (s_ // 2))
                w = gupk[sl4][:].rearrange("p a b -> p (a b)")
                T, S = dscT, dscS
                fv = T.bitcast(BF16)
                fv3 = fv.rearrange("p (m w) -> p m w", m=2)
                S3 = S.rearrange("p (m w) -> p m w", m=2)
                for ni, (msk, shl, sshl) in enumerate(LUT_SPEC):
                    gb_bf = plv_bf[psl][:, :, ni, :]
                    gb_u = plv_u16[psl][:, :, ni, :]
                    v_.tensor_scalar(T, w, msk, shl, ALU.bitwise_and,
                                     ALU.logical_shift_left)
                    v_.tensor_scalar(T, T, 16128, None, ALU.add)
                    v_.tensor_scalar(S, w, sshl, 0x8000, ALU.logical_shift_left,
                                     ALU.bitwise_and)
                    v_.tensor_scalar(gb_bf, fv3, 1.0, 0.0, ALU.subtract, ALU.min)
                    v_.tensor_tensor(gb_bf, gb_bf, fv3, ALU.add)
                    v_.tensor_tensor(gb_u, gb_u, S3, ALU.bitwise_xor)
                    for m in range(2):
                        sca = gscf[:, 2 * sl4 + m, :].unsqueeze(2).broadcast_to(
                            (128, 64, 16))
                        pm = plv_bf[psl][:, m, ni, :].rearrange(
                            "p (b r) -> p b r", r=16)
                        last = v_.tensor_tensor(pm, pm, sca, ALU.mult)
                inc(last, "v")
                C[f"_dq_{s_}"] = C["v"]

            def dn_dequant(v_, k):
                dsl = 2 if (k % GSETS) == 5 else (k % 2)
                wt(v_, "dDN%d" % (k % 2), 64 * (k // 2 + 1))
                if k >= 1:
                    wt(v_, "dU%d" % (((k - 1) // 2) % 2))
                w = dnpk[dsl][:].rearrange("p a b -> p (a b)")
                T, S = dscT[:, 0:1024], dscS[:, 0:1024]
                fv = T.bitcast(BF16)
                fv3 = fv.rearrange("p (m w) -> p m w", m=2)
                S3 = S.rearrange("p (m w) -> p m w", m=2)
                for ni, (msk, shl, sshl) in enumerate(LUT_SPEC):
                    gb_bf = dpl_bf[:, :, ni, :]
                    gb_u = dpl_u16[:, :, ni, :]
                    v_.tensor_scalar(T, w, msk, shl, ALU.bitwise_and,
                                     ALU.logical_shift_left)
                    v_.tensor_scalar(T, T, 16128, None, ALU.add)
                    v_.tensor_scalar(S, w, sshl, 0x8000, ALU.logical_shift_left,
                                     ALU.bitwise_and)
                    v_.tensor_scalar(gb_bf, fv3, 1.0, 0.0, ALU.subtract, ALU.min)
                    v_.tensor_tensor(gb_bf, gb_bf, fv3, ALU.add)
                    v_.tensor_tensor(gb_u, gb_u, S3, ALU.bitwise_xor)
                    for i2 in range(2):
                        sca = dscf[:, 2 * dsl + i2, :].unsqueeze(2).broadcast_to(
                            (128, 32, 16))
                        pm = dpl_bf[:, i2, ni, :].rearrange("p (b r) -> p b r", r=16)
                        last = v_.tensor_tensor(pm, pm, sca, ALU.mult)
                inc(last, "v")
                C[f"_dndq_{k}"] = C["v"]

            g_pk2 = g_pk.rearrange("(r two) w -> two r w", two=2)
            u_pk2 = u_pk.rearrange("(r two) w -> two r w", two=2)
            g_sc2 = g_sc.rearrange("(r two) w -> two r w", two=2)
            u_sc2 = u_sc.rearrange("(r two) w -> two r w", two=2)
            glsl = [glscr,
        qtab[:].rearrange("p a b c -> p (a b c)")[:, 0:256].bitcast(BF16)]

            def gu_loads(sy, t):
                slt = t % 4
                part, jt = (t % GSETS) % 2, (t % GSETS) // 2
                r0h = 512 * (t // GSETS) + 128 * jt
                sem = "dPK%d" % (t % 2)
                ring_dma(sy, out=gupk[slt][:, 0, :],
                         in_=g_pk2[part, r0h:r0h + 128, :]).then_inc(SEM[sem], 16)
                ring_dma(sy, out=gupk[slt][:, 1, :],
                         in_=u_pk2[part, r0h:r0h + 128, :]).then_inc(SEM[sem], 16)
                ring_dma(sy, out=gscf[:, 2 * slt + 0, :],
                         in_=g_sc2[part, r0h:r0h + 128, :]).then_inc(SEM[sem], 16)
                d1 = ring_dma(sy, out=gscf[:, 2 * slt + 1, :],
                              in_=u_sc2[part, r0h:r0h + 128, :])
                inc(d1, sem, 16)
                C[sem] += 48

            def dn_loads(sy, k):
                dsl = 2 if (k % GSETS) == 5 else (k % 2)
                sem = "dDN%d" % (k % 2)
                gi_ = k // GSETS
                hg_ = (k % GSETS) // 2
                hc0 = 4 * hg_ + 2 * (k % 2)
                for i2 in range(2):
                    hc = hc0 + i2
                    ring_dma(sy, out=dnpk[dsl][:, i2, :],
                             in_=d_pk[128 * hc:128 * (hc + 1),
                                      512 * gi_:512 * (gi_ + 1)]).then_inc(SEM[sem], 16)
                    d1 = ring_dma(sy, out=dscf[:, 2 * dsl + i2, :],
                                  in_=d_sc[128 * hc:128 * (hc + 1),
                                           32 * gi_:32 * (gi_ + 1)])
                    if i2 == 1:
                        inc(d1, sem, 16)
                        C[sem] += 48
                    else:
                        d1.then_inc(SEM[sem], 16)

            def dn_xbar_section(k):
                @block.sync
                def _(sy, k=k):
                    wt(sy, "v", C[f"_dndq_{k}"])
                    hgg = k // 2
                    ws2 = hgg % 2
                    sbr = k % 2
                    if hgg >= 2 and sbr == 0:
                        wt(sy, "t", C.get(f"_dmm_{hgg - 2}", 0))
                    for i2 in range(2):
                        hcL = 2 * sbr + i2
                        d1 = ring_dma(sy,
                                      out=WdT[ws2][:, :, 128 * hcL:128 * (hcL + 1)],
                                      in_=dplanes[:, i2, :], transpose=True)
                        if i2 == 1:
                            inc(d1, "dU%d" % ws2, 16)
                            C["dU%d" % ws2] += 16
                        else:
                            d1.then_inc(SEM["dU%d" % ws2], 16)

            def emit_evac(s_, th):
                u = 2 * s_ + th
                pu = pp[2 * (u % 3) + 1]

                @block.vector
                def _(v_, th=th, u=u, slo=s_ % GSETS, pu=pu, gi=s_ // GSETS):
                    wt(v_, "a", C[f"_gl_{u}"])
                    if gi >= 1 and u % (2 * GSETS) < 2:
                        wt(v_, "t", C.get(f"_dn_t_{gi - 1}", 0))
                    cv = v_.tensor_tensor(
                        inter[:, slo, 512 * th:512 * (th + 1)],
                        glsl[u % 2], pu[:], ALU.mult)
                    inc(cv, "v")
                    C[f"_ie_{u}"] = C["v"]

            for gi in range(NGRP):
                dn_on = gi in (DOWN_GRPS if DOWN_GRPS is not None else range(NGRP))
                for slo in range(GSETS):
                    s_ = GSETS * gi + slo
                    psl = s_ % 2
                    ws = s_ % 2

                    if s_ == 0:
                        @block.sync
                        def _(sy):
                            wt(sy, "v", C["_ln1_v"])
                            wt(sy, "dXB", C["_y_xb"])
                            for t in range(4):
                                gu_loads(sy, t)

                    if slo >= 1:
                        emit_evac(s_ - 1, 0)
                        emit_evac(s_ - 1, 1)

                    @block.vector
                    def _(v_, s_=s_):
                        gu_dequant(v_, s_)

                    @block.sync
                    def _(sy, s_=s_, psl=psl, ws=ws, dn_on=dn_on):
                        wt(sy, "v", C[f"_dq_{s_}"])
                        if s_ >= 2:
                            wt(sy, "t", C[f"_gu_{s_ - 2}_1"])
                        d1 = ring_dma(sy, out=WT[ws][:],
                                          in_=planes[psl][:].rearrange("p a b -> p (a b)"),
                                          transpose=True)
                        inc(d1, "dT%d" % ws, 16)
                        if s_ + 4 < GSETS * NGRP:
                            gu_loads(sy, s_ + 4)
                        if dn_on:
                            dn_loads(sy, s_)

                    if dn_on and slo <= 4:
                        @block.vector
                        def _(v_, s_=s_):
                            dn_dequant(v_, s_)

                    if dn_on and slo <= 3:
                        dn_xbar_section(s_)

                    for th in range(2):
                        u = 2 * s_ + th
                        pg = pp[2 * (u % 3)]
                        pu = pp[2 * (u % 3) + 1]

                        @block.tensor
                        def _(t_, s_=s_, th=th, u=u, ws=ws, pg=pg, pu=pu, gi=gi):
                            if th == 0:
                                wt(t_, "dT%d" % ws, 16 * (s_ // 2 + 1))
                                if s_ == 0:
                                    wt(t_, "dXB", C["_y_xb"])
                            prev = C.get(f"_ie_{u - 3}", 0)
                            if prev:
                                wt(t_, "v", prev)
                            for jj in range(8):
                                t_.matmul(pg[:], WT[ws][:, jj, :],
                                          yTe[:, jj, 512 * th:512 * (th + 1)],
                                          start=(jj == 0), stop=False)
                            for jj in range(8):
                                t_.matmul(pg[:], WT[ws][:, 8 + jj, :],
                                          yTo[:, jj, 512 * th:512 * (th + 1)],
                                          start=False, stop=(jj == 7))
                            for jj in range(8):
                                t_.matmul(pu[:], WT[ws][:, 16 + jj, :],
                                          yTe[:, jj, 512 * th:512 * (th + 1)],
                                          start=(jj == 0), stop=False)
                            for jj in range(8):
                                mm = t_.matmul(pu[:], WT[ws][:, 24 + jj, :],
                                               yTo[:, jj, 512 * th:512 * (th + 1)],
                                               start=False, stop=(jj == 7))
                            inc(mm, "t")
                            C[f"_gu_{s_}_{th}"] = C["t"]

                        @block.scalar
                        def _(s2, s_=s_, th=th, u=u, pg=pg):
                            wt(s2, "t", C[f"_gu_{s_}_{th}"])
                            prev = C.get(f"_ie_{u - 2}", 0)
                            if prev:
                                wt(s2, "v", prev)
                            a1 = s2.activation(glsl[u % 2], pg[:], AF.Gelu_apprx_tanh)
                            inc(a1, "a")
                            C[f"_gl_{u}"] = C["a"]

                emit_evac(GSETS * gi + GSETS - 1, 0)
                emit_evac(GSETS * gi + GSETS - 1, 1)

                if t_int is not None and gi == 0:
                    @block.sync
                    def _(sy, gi=gi):
                        wt(sy, "v", C[f"_ie_{2 * GSETS - 1}"])
                        inc(ring_dma(sy, out=t_int,
                                         in_=inter[:].rearrange("p a b -> p (a b)")),
                            "dST", 16)

                if not dn_on:
                    C[f"_dn_t_{gi}"] = C["t"]
                    for tt in range(NTT):
                        C[f"_de_{gi}_3_{tt}"] = C["v"]
                    continue
                for hg in range(4):
                    hgg = 4 * gi + hg
                    ws2 = hgg % 2
                    for tt in range(NTT):
                        @block.tensor
                        def _(t_, gi=gi, hg=hg, hgg=hgg, ws2=ws2, tt=tt):
                            if tt == 0:
                                wt(t_, "dU%d" % ws2, 64 * (hgg // 2 + 1))
                                wt(t_, "v", C[f"_ie_{2 * GSETS * (gi + 1) - 1}"])
                            if DBG_CB is not None and gi == 0 and hg == 0 and tt == 0:
                                t_.dbg_aps(DBG_CB, inter[:, :, 0:128],
                                           WdT[ws2][:])
                            pd = (pp_r, pp6)[tt % 2]
                            prev = C.get(f"_de_{gi}_{hg}_{tt - 2}" if tt >= 2 else
                                         (f"_de_{gi}_{hg - 1}_{NTT - 2 + tt}" if hg >= 1 else
                                          (f"_de_{gi - 1}_3_{NTT - 2 + tt}" if gi >= 1
                                           else "_none")), 0)
                            if prev:
                                t_.wait_ge(SEM["v"], prev)
                            for slo in range(GSETS):
                                par, j = slo % 2, slo // 2
                                mm = t_.matmul(pd[:], inter[:, slo, 128 * tt:128 * (tt + 1)],
                                               WdT[ws2][:, 4 * par + j, :],
                                               start=(slo == 0), stop=(slo == GSETS - 1))
                            inc(mm, "t")
                            C[f"_dmmt_{hgg}_{tt}"] = C["t"]
                            if DBG_CB is not None and gi == 0 and hg == 0 and tt == 0:
                                t_.dbg_aps(DBG_CB, pd[:])

                        @block.vector
                        def _(v_, gi=gi, hg=hg, hgg=hgg, tt=tt):
                            pd = (pp_r, pp6)[tt % 2]
                            wt(v_, "t", C[f"_dmmt_{hgg}_{tt}"])
                            if DBG_CB is not None and gi == 0 and hg == 0 and tt == 0:
                                v_.dbg_aps(DBG_CB, pd[:], e_t[tt][:, 0:128])
                            cv = v_.tensor_tensor(e_t[tt][:, 512 * hg:512 * (hg + 1)],
                                                  e_t[tt][:, 512 * hg:512 * (hg + 1)],
                                                  pd[:], ALU.add)
                            inc(cv, "v")
                            C[f"_de_{gi}_{hg}_{tt}"] = C["v"]
                    C[f"_dmm_{hgg}"] = C["t"]
                    if hg < 2:
                        k1 = GSETS * gi + 4 + 2 * hg
                        dn_xbar_section(k1)
                        for kq in (k1 + 1, k1 + 2):
                            if kq < GSETS * gi + GSETS:
                                @block.vector
                                def _(v_, k=kq):
                                    dn_dequant(v_, k)
                                if kq == k1 + 1:
                                    dn_xbar_section(kq)
                C[f"_dn_t_{gi}"] = C["t"]

        if t_e2 is not None and stage >= 6:
            for tt in range(NTT):
                @block.sync
                def _(sy, tt=tt):
                    wt(sy, "v", C[f"_de_{NGRP - 1}_3_{tt}"])
                    inc(ring_dma(sy, out=t_e2[128 * tt:128 * (tt + 1), :],
                                 in_=e_t[tt][:]), "dST", 16)

        if stage >= 7:
            for tt in range(NTT):
                @block.scalar
                def _(s_, tt=tt):
                    s_.wait_ge(SEM["v"], C[f"_de_{NGRP - 1}_3_{tt}"])
                    if tt >= 1:
                        s_.wait_ge(SEM["v"], C[f"_l2s_{tt - 1}"])
                    a1 = s_.activation(lnscr[:], e_t[tt][:], AF.Square)
                    inc(a1, "a")
                    C[f"_l2q_{tt}"] = C["a"]

                @block.vector
                def _(v_, tt=tt):
                    wt(v_, "v", C[f"_de_{NGRP - 1}_3_{tt}"])
                    wt(v_, "a", C[f"_l2q_{tt}"])
                    et = e_t[tt][:]
                    VC(v_, v_.tensor_reduce(stat[:, 0:1], et, axis=AX.X, op=ALU.add))
                    VC(v_, v_.tensor_reduce(stat[:, 1:2], lnscr[:], axis=AX.X, op=ALU.add))
                    VC(v_, v_.tensor_scalar(stat[:, 0:1], stat[:, 0:1], 1.0 / D, None, ALU.mult))
                    VC(v_, v_.tensor_scalar(stat[:, 1:2], stat[:, 1:2], 1.0 / D, None, ALU.mult))
                    VC(v_, v_.tensor_scalar(stat[:, 2:3], stat[:, 0:1], stat[:, 0:1],
                                            None, ALU.mult))
                    cv = v_.tensor_tensor(stat[:, 1:2], stat[:, 1:2], stat[:, 2:3],
                                          ALU.subtract)
                    inc(cv, "v")
                    C[f"_l2s_{tt}"] = C["v"]

                @block.scalar
                def _(s_, tt=tt):
                    wt(s_, "v", C[f"_l2s_{tt}"])
                    aln = s_.activation(stat[:, 2:3], stat[:, 1:2], AF.Ln, bias=epsln[:, 0:1])
                    inc(aln, "a")
                    wt(s_, "a")
                    a1 = s_.activation(stat[:, 3:4], stat[:, 2:3], AF.Exp, scale=-0.5)
                    inc(a1, "a")
                    C[f"_l2r_{tt}"] = C["a"]

                @block.vector
                def _(v_, tt=tt):
                    wt(v_, "a", C[f"_l2r_{tt}"])
                    cv = v_.tensor_scalar(e_t[tt][:], e_t[tt][:], stat[:, 0:1],
                                          stat[:, 3:4], ALU.subtract, ALU.mult)
                    inc(cv, "v")
                    C[f"_l2a_{tt}"] = C["v"]

                @block.sync
                def _(sy, tt=tt):
                    wt(sy, "v", C[f"_l2a_{tt}"])
                    d1 = ring_dma(sy, out=out[128 * tt:128 * (tt + 1), :],
                                      in_=e_t[tt][:])
                    inc(d1, "dST", 16)
        else:
            @block.sync
            def _(sy):
                for tt in range(NTT):
                    d1 = ring_dma(sy, out=out[128 * tt:128 * (tt + 1), :],
                                      in_=e_t[tt][:])
                    inc(d1, "dST", 16)

        @block.sync
        def _(sy):
            wt(sy, "dST")

    return nc, tap_aps


def _shards(inputs):
    x = np.asarray(inputs["L0_attn_x"])
    qkw = np.ascontiguousarray(np.asarray(inputs["L0_attn_q_kernel"]))
    kvw = np.ascontiguousarray(np.asarray(inputs["L0_attn_kv_kernel"]))
    okw = np.ascontiguousarray(np.asarray(inputs["L0_attn_out_kernel"]))
    qnsv = np.ascontiguousarray(np.asarray(inputs["L0_attn_q_norm_scale"])).reshape(1, HD)
    knsv = np.ascontiguousarray(np.asarray(inputs["L0_attn_k_norm_scale"])).reshape(1, HD)
    gpk = np.asarray(inputs["L0_ffn_gate_W_packed"]).astype(np.uint16)
    gsc = np.ascontiguousarray(np.asarray(inputs["L0_ffn_gate_scales"]))
    upk = np.asarray(inputs["L0_ffn_up_W_packed"]).astype(np.uint16)
    usc = np.ascontiguousarray(np.asarray(inputs["L0_ffn_up_scales"]))
    dpk = np.asarray(inputs["L0_ffn_down_W_packed"]).astype(np.uint16)
    dsc = np.ascontiguousarray(np.asarray(inputs["L0_ffn_down_scales"]))
    invts = np.ascontiguousarray(np.broadcast_to(
        (1.0 / (10000.0 ** (np.arange(64, dtype=np.float64) / 64.0))
         ).astype(np.float32)[None, :], (128, 64)))
    maps = []
    for core in range(8):
        b, h = core // 2, core % 2
        if h == 0:
            xkv = np.concatenate([np.zeros((1024, D), np.float32), x[b, 0:1024]],
                                 axis=0)
            val = np.concatenate([np.full(1024, -1e30, np.float32),
                                  np.zeros(1024, np.float32)])
            kof = np.full(128, -1024.0, np.float32)
        else:
            xkv = np.ascontiguousarray(x[b])
            val = np.zeros(2048, np.float32)
            kof = np.zeros(128, np.float32)
        maps.append({
            "x_kv": xkv, "val_in": val, "koff_in": kof, "invts_in": invts,
            "qkw": qkw, "kvk": kvw, "okw": okw, "qns": qnsv, "kns": knsv,
            "g_pk": gpk, "g_sc": gsc, "u_pk": upk, "u_sc": usc,
            "d_pk": dpk, "d_sc": dsc,
        })
    return maps


def kernel(**inputs):
    maps = _shards(inputs)
    nc, _ = build_program(taps=TAPS, stage=STAGE)
    res = run_bass_kernel_spmd(nc, maps, core_ids=list(range(8)))
    outp = np.empty((4, 2048, 2048), np.float32)
    for core in range(8):
        b, h = core // 2, core % 2
        outp[b, 1024 * h:1024 * (h + 1), :] = res.results[core]["out"]
    return outp


# version 04 (final)
# speedup vs baseline: 1.6746x; 1.3287x over previous
import numpy as np
import ml_dtypes
import concourse.bass as bass
import concourse.mybir as mybir
from concourse.bass_utils import run_bass_kernel_spmd
from contextlib import ExitStack

F32 = mybir.dt.float32
BF16 = mybir.dt.bfloat16
U16 = mybir.dt.uint16
I32 = mybir.dt.int32
AF = mybir.ActivationFunctionType
ALU = mybir.AluOpType
AX = mybir.AxisListType

TQ, TK, D = 1024, 2048, 2048
NH, KH, HD = 16, 8, 128
DC, NTT, NSC = 16, 8, 16
MLP = 16384
GSETS = 8
NGRP = 16
ATTN_SCALE = 0.08838834764831845
TWO_PI = float(2 * np.pi)
HALF_PI = float(np.pi / 2)

TAPS = ()
STAGE = 99
DOWN_GRPS = None
DBG_CB = None

K1 = 1024


def build_program(taps=(), stage=99):
    nc = bass.Bass("TRN2", target_bir_lowering=False, debug=(DBG_CB is not None))

    x_kv = nc.dram_tensor("x_kv", [TK, D], F32, kind="ExternalInput").ap()
    val_in = nc.dram_tensor("val_in", [TK], F32, kind="ExternalInput").ap()
    koff_in = nc.dram_tensor("koff_in", [128], F32, kind="ExternalInput").ap()
    invts_in = nc.dram_tensor("invts_in", [128, 64], F32, kind="ExternalInput").ap()
    qkw = nc.dram_tensor("qkw", [NH, D, HD], F32, kind="ExternalInput").ap()
    kvk = nc.dram_tensor("kvk", [2, KH, D, HD], F32, kind="ExternalInput").ap()
    okw = nc.dram_tensor("okw", [NH, HD, D], F32, kind="ExternalInput").ap()
    qns = nc.dram_tensor("qns", [1, HD], F32, kind="ExternalInput").ap()
    kns = nc.dram_tensor("kns", [1, HD], F32, kind="ExternalInput").ap()
    g_pk = nc.dram_tensor("g_pk", [MLP, D // 2], U16, kind="ExternalInput").ap()
    g_sc = nc.dram_tensor("g_sc", [MLP, 64], F32, kind="ExternalInput").ap()
    u_pk = nc.dram_tensor("u_pk", [MLP, D // 2], U16, kind="ExternalInput").ap()
    u_sc = nc.dram_tensor("u_sc", [MLP, 64], F32, kind="ExternalInput").ap()
    d_pk = nc.dram_tensor("d_pk", [D, MLP // 2], U16, kind="ExternalInput").ap()
    d_sc = nc.dram_tensor("d_sc", [D, MLP // 32], F32, kind="ExternalInput").ap()
    out = nc.dram_tensor("out", [TQ, D], F32, kind="ExternalOutput").ap()

    tap_aps = {}

    def tap(name, shape, dtype):
        if name in taps:
            tap_aps[name] = nc.dram_tensor("tap_" + name, shape, dtype,
                                           kind="ExternalOutput").ap()
            return tap_aps[name]
        return None

    t_xt = tap("xt", [128, DC * TK], BF16)
    t_rop0 = tap("rop0", [128, 2048], F32)
    t_ktb = tap("ktb", [128, 4096], F32)
    t_dbg = tap("dbg", [128, 1024], F32)
    t_kt = tap("kt", [128, KH * TK], BF16)
    t_v = tap("v", [128, NSC * KH * HD], BF16)
    t_qt = tap("qt", [128, NH * TQ], BF16)
    t_enc = tap("enc", [128, NH * TQ], BF16)
    t_y = tap("y", [TQ, D], F32)
    t_int = tap("int", [128, GSETS * TQ], BF16)
    t_e2 = tap("e2", [TQ, D], F32)

    es = ExitStack()
    with es:
        def sb(name, shape, dt):
            return es.enter_context(nc.sbuf_tensor(name, shape, dt))

        def psmt(name, shape):
            return es.enter_context(nc.psum_tensor(name, shape, F32))

        scr = sb("scr", [128, 1024], F32)
        qtab = sb("qtab", [128, 8, 4, 64], F32)
        stat = sb("stat", [128, 16], F32)
        prjw = sb("prj", [128, 1024], F32)
        prj = prjw[:, 0:512]
        rop = sb("rop", [128, 512], BF16)
        rbc = sb("rbc", [128, 512], F32)
        valv = sb("valv", [128, NSC], F32)
        koff = sb("koff", [128, 1], F32)
        invts = sb("invts", [128, 64], F32)
        piota = sb("piota", [128, 1], I32)
        piof = sb("piof", [128, 1], F32)
        onesb = sb("onesb", [128, 1], BF16)
        onesf1 = sb("onesf1", [1, 128], F32)
        epsln = sb("epsln", [128, 1], F32)
        epsrm = sb("epsrm", [128, 1], F32)
        qksc = sb("qksc", [1, 256], F32)

        AR = sb("AR", [128, 92 * K1], U16)

        def AV(o_k1, n_u16, dt, shape=None):
            ap = AR[:, o_k1 * K1:o_k1 * K1 + n_u16].bitcast(dt)
            if shape is not None:
                lead = "p (" + " ".join(f"a{i}" for i in range(len(shape))) + ") -> p " + \
                   " ".join(f"a{i}" for i in range(len(shape)))
                kw = {f"a{i}": s for i, s in enumerate(shape)}
                ap = ap.rearrange(lead, **kw)
            return ap

        kT = AV(0, 16 * K1, BF16, (KH, TK))
        masks = AV(32, 4 * K1, BF16, (8, 512))
        v_arr = AV(16, 16 * K1, BF16, (NSC, KH, HD))
        e_t = [AV(4 * tt, 4 * K1, F32, (D,)) for tt in range(8)]
        xT_q = AV(32, 16 * K1, BF16, (DC, TQ))
        wo_grp = [AV(32 + 8 * i, 8 * K1, BF16, (NH, 512)) for i in range(2)]
        planes = [AV(32 + 4 * i, 4 * K1, BF16, (4, 1024)) for i in range(2)]
        WT = [AV(40 + 4 * i, 4 * K1, BF16, (32, 128)) for i in range(2)]
        xT_k = AV(48, 16 * K1, BF16, (DC, TQ))
        qT = AV(48, 16 * K1, BF16, (NH, TQ))
        inter = AV(48, 8 * K1, BF16, (GSETS, TQ))
        WdT = [AV(56, 4 * K1, BF16, (8, 512)), AV(60, 4 * K1, BF16, (8, 512))]
        ropet = AV(64, 4 * K1, F32, (32, 64))
        wgrp = [AV(64 + 8 * i, 8 * K1, BF16, (DC, 512)) for i in range(2)]
        enc = AV(64, 16 * K1, BF16, (NH, TQ))
        yTe = AV(64, 8 * K1, BF16, (8, TQ))
        yTo = AV(72, 8 * K1, BF16, (8, TQ))
        xf32 = [AV(80 + 4 * i, 4 * K1, F32, (D,)) for i in range(2)]
        ktab = AV(80, 8 * K1, F32, (16, 4, 64))
        ptil = AV(80, 6 * K1, BF16, (12, 512))
        lnscr = AV(80, 4 * K1, F32, (D,))
        gupk = [AV(80 + 2 * i, 2 * K1, U16, (2, 1024)) for i in range(4)]
        xbf = [AV(88 + 2 * i, 2 * K1, BF16, (D,)) for i in range(2)]
        yev = AV(88, 2 * K1, BF16, (D,))
        wstg = AV(88, 2 * K1, F32, (1024,))
        dnp2 = qtab[:].rearrange("p a b c -> p (a b c)")[:, 1536:2048].bitcast(
            U16).rearrange("p (a b) -> p a b", a=2)
        dnpk = [AV(88 + i, 1 * K1, U16, (2, 512)) for i in range(2)] + [dnp2]
        dplanes = AV(90, 2 * K1, BF16, (2, 1024))

        dscT = scr[:].bitcast(U16)
        dscS = prjw[:].bitcast(U16)
        plv_bf = [AV(32 + 4 * i, 4 * K1, BF16, (2, 2, 1024)) for i in range(2)]
        plv_u16 = [AV(32 + 4 * i, 4 * K1, U16, (2, 2, 1024)) for i in range(2)]
        dpl_bf = AV(90, 2 * K1, BF16, (2, 2, 512))
        dpl_u16 = AV(90, 2 * K1, U16, (2, 2, 512))
        glscr = rop[:]
        gscf = sb("gscf", [128, 8, 64], F32)
        dscf = sb("dscf", [128, 6, 32], F32)

        pp = [psmt(f"pp{i}", [128, 512]) for i in range(6)]
        pp_r = psmt("pp_r", [128, 512])
        pp6 = psmt("pp6", [128, 512])

        SEM, C = {}, {}
        for s in ("dC", "dX0", "dX1", "dW0", "dW1", "dPK0", "dPK1", "dDN0",
                  "dDN1", "dT0", "dT1", "dU0", "dU1", "dXB", "dST", "v", "a",
                  "t", "p"):
            SEM[s] = es.enter_context(nc.semaphore(s))
            C[s] = 0

        def inc(instr, s, k=1):
            instr.then_inc(SEM[s], k)
            C[s] += k

        def wt(eng, s, v=None):
            eng.wait_ge(SEM[s], C[s] if v is None else v)

        DSEMS = ("dC", "dX0", "dX1", "dW0", "dW1", "dPK0", "dPK1", "dDN0",
                 "dDN1", "dT0", "dT1", "dU0", "dU1", "dXB", "dST")
        _waited = {sname: 0 for sname in DSEMS}
        _rstate = {"xp": False}

        def ring_sync(sy):
            for sname in DSEMS:
                if C[sname] > _waited[sname]:
                    sy.wait_ge(SEM[sname], C[sname])
                    _waited[sname] = C[sname]

        def VC(v_, instr):
            inc(instr, "v")
            wt(v_, "v")

        def ring_dma(sy, *a, transpose=False, **kw):
            if transpose or _rstate["xp"]:
                ring_sync(sy)
            _rstate["xp"] = transpose
            return sy.dma_start(*a, transpose=transpose, **kw)

        block = es.enter_context(nc.Block())

        @block.sync
        def _(sy):
            for d1 in (
                ring_dma(sy, out=koff[:], in_=koff_in.rearrange("(p o) -> p o", o=1)),
                ring_dma(sy, out=invts[:], in_=invts_in),
                ring_dma(sy, out=valv[:], in_=val_in.rearrange("(c p) -> p c", p=128),
                             allow_slow_non_contiguous=True),
                ring_dma(sy, out=qksc[:, 0:128], in_=qns),
                ring_dma(sy, out=qksc[:, 128:256], in_=kns),
            ):
                inc(d1, "dC", 16)

        @block.gpsimd
        def _(g_):
            g_.iota(piota[:], pattern=[[0, 1]], base=0, channel_multiplier=1)
            inc(g_.memset(onesb[:], 1.0), "p")

        @block.vector
        def _(v_):
            wt(v_, "dC")
            wt(v_, "p")
            VC(v_, v_.tensor_copy(piof[:], piota[:]))
            v_.memset(onesf1[:], 1.0)
            v_.memset(epsln[:], 1e-5)
            VC(v_, v_.memset(epsrm[:], 1e-6))
            arg = scr[:, 0:64]
            n1 = scr[:, 64:128]
            ni = scr[:, 128:192].bitcast(I32)[:, 0:64]
            posf = scr[:, 192:193]
            for c in range(16):
                VC(v_, v_.tensor_scalar(posf, piota[:], koff[:, 0:1], float(128 * c),
                                        ALU.add, ALU.add))
                VC(v_, v_.tensor_scalar(arg, invts[:], posf, None, ALU.mult))
                for j, shift in ((0, 0.0), (1, HALF_PI)):
                    dst = ropet[:, 2 * c + j, :]
                    VC(v_, v_.tensor_scalar(n1, arg, shift, 1.0 / TWO_PI, ALU.add, ALU.mult))
                    VC(v_, v_.tensor_copy(ni, n1))
                    VC(v_, v_.tensor_scalar(n1, arg, shift, None, ALU.add))
                    VC(v_, v_.scalar_tensor_tensor(dst, ni, -TWO_PI, n1, ALU.mult, ALU.add))
                    VC(v_, v_.tensor_scalar(n1, dst, float(np.pi), None, ALU.is_gt))
                    VC(v_, v_.scalar_tensor_tensor(dst, n1, -TWO_PI, dst, ALU.mult, ALU.add))
                    VC(v_, v_.tensor_scalar(n1, dst, float(-np.pi), None, ALU.is_lt))
                    VC(v_, v_.scalar_tensor_tensor(dst, n1, TWO_PI, dst, ALU.mult, ALU.add))
            iv = v_.tensor_copy(scr[:, 256:257], piof[:])
            inc(iv, "v")
        v_args = C["v"]

        @block.scalar
        def _(s_):
            wt(s_, "v", v_args)
            a1 = s_.activation(ropet[:].rearrange("p a b -> p (a b)"),
                               ropet[:].rearrange("p a b -> p (a b)"), AF.Sin)
            inc(a1, "a")
        a_sin = C["a"]

        @block.tensor
        def _(t_):
            wt(t_, "a", a_sin)
            t_.matmul(pp[0][:, 0:128], onesf1[:], qksc[0:1, 0:128], start=True, stop=True)
            tm = t_.matmul(pp[0][:, 128:256], onesf1[:], qksc[0:1, 128:256],
                           start=True, stop=True)
            inc(tm, "t")
        t_scb = C["t"]

        @block.vector
        def _(v_):
            wt(v_, "t", t_scb)
            qsR = scr[:, 0:128]
            ksR = scr[:, 128:256]
            VC(v_, v_.tensor_copy(qsR, pp[0][:, 0:128]))
            VC(v_, v_.tensor_copy(ksR, pp[0][:, 128:256]))
            for c in range(8, 16):
                sinc, cosc = ropet[:, 2 * c, :], ropet[:, 2 * c + 1, :]
                v_.tensor_tensor(qtab[:, c - 8, 0, :], cosc, qsR[:, 0:64], ALU.mult)
                v_.tensor_tensor(qtab[:, c - 8, 1, :], cosc, qsR[:, 64:128], ALU.mult)
                v_.tensor_tensor(qtab[:, c - 8, 2, :], sinc, qsR[:, 0:64], ALU.mult)
                kv4 = v_.tensor_tensor(qtab[:, c - 8, 3, :], sinc, qsR[:, 64:128],
                                       ALU.mult)
            VC(v_, kv4)
            iv2 = v_.tensor_copy(scr[:, 256:257], piof[:])
            inc(iv2, "v")
        v_tabs = C["v"]

        for sc_i in range(NSC):
            sl = sc_i % 2

            @block.sync
            def _(sy, sc_i=sc_i, sl=sl):
                if sc_i >= 2:
                    wt(sy, "v", C[f"_xc{sc_i - 2}"])
                d1 = ring_dma(sy, out=xf32[sl][:],
                                  in_=x_kv[128 * sc_i:128 * (sc_i + 1), :])
                inc(d1, "dX%d" % sl, 16)

            @block.vector
            def _(v_, sc_i=sc_i, sl=sl):
                wt(v_, "dX%d" % sl, 16 * (sc_i // 2 + 1))
                if sc_i == 0:
                    wt(v_, "v", v_tabs)
                if sc_i >= 2:
                    wt(v_, "dXB", 16 * (sc_i - 1))
                cv = v_.tensor_copy(xbf[sl][:], xf32[sl][:])
                inc(cv, "v")
                C[f"_xc{sc_i}"] = C["v"]

            @block.sync
            def _(sy, sc_i=sc_i, sl=sl):
                wt(sy, "v", C[f"_xc{sc_i}"])
                dst = xT_k if sc_i < 8 else xT_q
                so = sc_i if sc_i < 8 else sc_i - 8
                d1 = ring_dma(sy, out=dst[:, :, 128 * so:128 * (so + 1)],
                                  in_=xbf[sl][:], transpose=True)
                inc(d1, "dXB", 16)
        dXB_xt = C["dXB"]
        C["_p1_v"] = C["v"]

        @block.vector
        def _(v_):
            wt(v_, "dXB", dXB_xt)
            ksR = scr[:, 128:256]
            for c in range(16):
                sinc, cosc = ropet[:, 2 * c, :], ropet[:, 2 * c + 1, :]
                v_.tensor_tensor(ktab[:, c, 0, :], cosc, ksR[:, 0:64], ALU.mult)
                v_.tensor_tensor(ktab[:, c, 1, :], cosc, ksR[:, 64:128], ALU.mult)
                v_.tensor_tensor(ktab[:, c, 2, :], sinc, ksR[:, 0:64], ALU.mult)
                kv4 = v_.tensor_tensor(ktab[:, c, 3, :], sinc, ksR[:, 64:128], ALU.mult)
            VC(v_, kv4)
            iv3 = v_.tensor_copy(scr[:, 256:257], piof[:])
            inc(iv3, "v")
        v_ktab = C["v"]

        if t_rop0 is not None:
            @block.sync
            def _(sy):
                wt(sy, "v", v_ktab)
                inc(ring_dma(sy, out=t_rop0, in_=ropet[:].rearrange("p a b -> p (a b)")), "dST", 16)
                inc(ring_dma(sy, out=t_ktb, in_=ktab[:].rearrange("p a b c -> p (a b c)")), "dST", 16)

        if t_xt is not None:
            @block.sync
            def _(sy):
                wt(sy, "dXB", dXB_xt)
                d0 = ring_dma(sy, out=t_xt.rearrange("p (d t) -> p d t", d=DC)[:, :, 0:1024],
                                  in_=xT_k[:])
                inc(d0, "dST", 16)
                d1 = ring_dma(sy, 
                    out=t_xt.rearrange("p (d t) -> p d t", d=DC)[:, :, 1024:2048],
                    in_=xT_q[:])
                inc(d1, "dST", 16)

        def load_wgrp(dst, col_aps, vkey, extra_ring_wait=None, cast_wait=None):
            nb = len(col_aps)
            for i_ in range(nb):
                wsl = i_ % 2

                @block.sync
                def _(sy, i_=i_, wsl=wsl):
                    if extra_ring_wait is not None and i_ == 0:
                        for sem_name, valw in extra_ring_wait:
                            sy.wait_ge(SEM[sem_name], valw)
                    if i_ >= 2:
                        wt(sy, "v", C[f"_wc_{vkey}_{i_ - 2}"])
                    dst_ap = wstg[:, 512 * wsl:512 * (wsl + 1)]
                    if col_aps[i_].ndim == 3:
                        dst_ap = dst_ap.rearrange("p (a b) -> p a b",
                                                  b=col_aps[i_].shape[2])
                    d1 = ring_dma(sy, out=dst_ap, in_=col_aps[i_])
                    inc(d1, "dW%d" % wsl, 16)

                @block.vector
                def _(v_, i_=i_, wsl=wsl):
                    wt(v_, "dW%d" % wsl, C["dW%d" % wsl])
                    if cast_wait is not None and i_ == 0:
                        v_.wait_ge(SEM[cast_wait[0]], cast_wait[1])
                    cv = v_.tensor_copy(dst(i_), wstg[:, 512 * wsl:512 * (wsl + 1)])
                    inc(cv, "v")
                    C[f"_wc_{vkey}_{i_}"] = C["v"]
            C[f"_wg_{vkey}"] = C["v"]

        def kv_cols(c, gidx):
            cc, gg = gidx // 2, gidx % 2
            return kvk[cc, :, :, :].rearrange("k d h -> d k h")[
                128 * c:128 * (c + 1), 4 * gg:4 * (gg + 1), :]

        def q_cols(c, g):
            return qkw.rearrange("n d h -> d n h")[128 * c:128 * (c + 1),
                                                   4 * g:4 * (g + 1), :]

        kv_order = [2, 3, 0, 1]
        load_wgrp(lambda i: wgrp[0][:, i, :], [kv_cols(c, 2) for c in range(DC)],
                  "kv0", extra_ring_wait=[("v", v_ktab)])
        for gi in range(4):
            gidx = kv_order[gi]
            gslot = gi % 2
            if gi + 1 < 4:
                load_wgrp(lambda i, s=1 - gslot: wgrp[s][:, i, :],
                          [kv_cols(c, kv_order[gi + 1]) for c in range(DC)],
                          f"kv{gi + 1}",
                          extra_ring_wait=[("v", C[f"_wg_kv{gi}"])],
                          cast_wait=(("t", C[f"_kvp_{gi - 1}_15"]) if gi >= 1
                                     else None))
            wgready = C[f"_wg_kv{gi}"]
            for st in range(NSC):
                ppj = pp[st % 2]
                tmark = f"_kvp_{gi}_{st}"

                @block.tensor
                def _(t_, st=st, ppj=ppj, gslot=gslot, wgready=wgready, gi=gi, tmark=tmark):
                    wt(t_, "v", wgready)
                    if gi == 0 and st < 2:
                        wt(t_, "dXB", dXB_xt)
                    prev = C.get(f"_kve_{gi}_{st - 2}" if st >= 2 else
                                 f"_kve_{gi - 1}_{NSC - 2 + st}", 0)
                    if prev:
                        wt(t_, "v", prev)
                    xsrc = xT_k if st < 8 else xT_q
                    so = st if st < 8 else st - 8
                    for dc in range(DC):
                        mm = t_.matmul(ppj[:], xsrc[:, dc, 128 * so:128 * (so + 1)],
                                       wgrp[gslot][:, dc, :],
                                       start=(dc == 0), stop=(dc == DC - 1))
                    inc(mm, "t")
                    C[tmark] = C["t"]

                if gidx >= 2:
                    gg = gidx - 2

                    @block.vector
                    def _(v_, st=st, ppj=ppj, gg=gg, gi=gi, tmark=tmark):
                        wt(v_, "t", C[tmark])
                        cv = v_.tensor_copy(
                            v_arr[:, st, 4 * gg:4 * gg + 4, :].rearrange(
                                "p a b -> p (a b)"), ppj[:])
                        inc(cv, "v")
                        C[f"_kve_{gi}_{st}"] = C["v"]
                else:
                    gg = gidx

                    @block.vector
                    def _(v_, st=st, ppj=ppj, gg=gg, gi=gi, tmark=tmark):
                        wt(v_, "t", C[tmark])
                        cv = v_.tensor_copy(prj[:], ppj[:])
                        inc(cv, "v")
                        C[f"_kev_{gi}_{st}"] = C["v"]

                    @block.scalar
                    def _(s_, gi=gi, st=st):
                        wt(s_, "v", C[f"_kev_{gi}_{st}"])
                        a1 = s_.activation(rbc[:], prj[:], AF.Square)
                        inc(a1, "a")
                        C[f"_ksa_{gi}_{st}"] = C["a"]

                    @block.vector
                    def _(v_, gi=gi, st=st):
                        wt(v_, "a", C[f"_ksa_{gi}_{st}"])
                        rv = v_.tensor_reduce(stat[:, 0:4],
                                              rbc[:].rearrange("p (h x) -> p h x", x=128),
                                              axis=AX.X, op=ALU.add)
                        inc(rv, "v")
                        C[f"_krms_{gi}_{st}"] = C["v"]

                    @block.scalar
                    def _(s_, gi=gi, st=st):
                        wt(s_, "v", C[f"_krms_{gi}_{st}"])
                        a1 = s_.activation(stat[:, 4:8], stat[:, 0:4], AF.Sqrt,
                                           scale=1.0 / 128.0, bias=epsrm[:, 0:1])
                        inc(a1, "a")
                        C[f"_ksq_{gi}_{st}"] = C["a"]

                    @block.vector
                    def _(v_, st=st, gg=gg, gi=gi, ppj=ppj):
                        wt(v_, "a", C[f"_ksq_{gi}_{st}"])
                        wt(v_, "dXB")
                        VC(v_, v_.reciprocal(stat[:, 0:4], stat[:, 4:8]))
                        for hh in range(4):
                            iap = v_.tensor_scalar(prj[:, 128 * hh:128 * (hh + 1)],
                                                   prj[:, 128 * hh:128 * (hh + 1)],
                                                   stat[:, hh:hh + 1], None, ALU.mult)
                            if hh == 3:
                                VC(v_, iap)
                        x1 = prj[:].rearrange("p (h x) -> p h x", x=128)[:, :, 0:64]
                        x2 = prj[:].rearrange("p (h x) -> p h x", x=128)[:, :, 64:128]
                        o1 = rop[:].rearrange("p (h x) -> p h x", x=128)[:, :, 0:64]
                        o2 = rop[:].rearrange("p (h x) -> p h x", x=128)[:, :, 64:128]
                        tA = scr[:, 512:768].rearrange("p (h x) -> p h x", x=64)
                        tB = scr[:, 768:1024].rearrange("p (h x) -> p h x", x=64)

                        def tb(j):
                            return ktab[:, st, j, :].unsqueeze(1).broadcast_to((128, 4, 64))
                        VC(v_, v_.tensor_tensor(tA, x1, tb(0), ALU.mult))
                        VC(v_, v_.tensor_tensor(tB, x2, tb(3), ALU.mult))
                        VC(v_, v_.tensor_tensor(o1, tA, tB, ALU.subtract))
                        VC(v_, v_.tensor_tensor(tA, x2, tb(1), ALU.mult))
                        VC(v_, v_.tensor_tensor(tB, x1, tb(2), ALU.mult))
                        cv = v_.tensor_tensor(o2, tA, tB, ALU.add)
                        inc(cv, "v")
                        C[f"_kve_{gi}_{st}"] = C["v"]

                    @block.sync
                    def _(sy, st=st, gg=gg, gi=gi):
                        wt(sy, "v", C[f"_kve_{gi}_{st}"])
                        d1 = ring_dma(sy, 
                            out=kT[:, 4 * gg:4 * gg + 4, 128 * st:128 * (st + 1)],
                            in_=rop[:], transpose=True)
                        inc(d1, "dXB", 16)
        C["_kv_v"] = C["v"]
        C["_kv_t"] = C["t"]
        dXB_k = C["dXB"]

        if t_dbg is not None:
            @block.sync
            def _(sy):
                wt(sy, "v", C["_kv_v"])
                inc(ring_dma(sy, out=t_dbg[:, 0:16], in_=stat[:]), "dST", 16)
                inc(ring_dma(sy, out=t_dbg[:, 16:528], in_=prj[:]), "dST", 16)
                inc(ring_dma(sy, out=t_dbg[:, 528:1040] if False else t_dbg[:, 512:1024], in_=rbc[:]), "dST", 16)

        if t_kt is not None:
            @block.sync
            def _(sy):
                wt(sy, "dXB", dXB_k)
                inc(ring_dma(sy, out=t_kt, in_=kT[:].rearrange("p a b -> p (a b)")),
                    "dST", 16)
        if t_v is not None:
            @block.sync
            def _(sy):
                wt(sy, "v", C["_kv_v"])
                inc(ring_dma(sy, out=t_v,
                                 in_=v_arr[:].rearrange("p a b c -> p (a b c)")),
                    "dST", 16)

        if stage >= 3:
            load_wgrp(lambda i: wgrp[0][:, i, :], [q_cols(c, 0) for c in range(DC)],
                      "q0", extra_ring_wait=[("v", C["_wg_kv3"])],
                      cast_wait=("t", C["_kvp_2_15"]))
            for gq in range(4):
                gslot = gq % 2
                if gq + 1 < 4:
                    load_wgrp(lambda i, s=1 - gslot: wgrp[s][:, i, :],
                              [q_cols(c, gq + 1) for c in range(DC)], f"q{gq + 1}",
                              extra_ring_wait=[("v", C[f"_wg_q{gq}"])],
                              cast_wait=(("t", C[f"_qp_{gq - 1}_7"]) if gq >= 1
                                         else ("t", C["_kvp_3_15"])))
                wgready = C[f"_wg_q{gq}"]
                for tt in range(NTT):
                    ppj = pp[tt % 2]
                    tmark = f"_qp_{gq}_{tt}"

                    @block.tensor
                    def _(t_, tt=tt, ppj=ppj, gslot=gslot, wgready=wgready, gq=gq, tmark=tmark):
                        wt(t_, "v", wgready)
                        if gq == 0 and tt < 2:
                            wt(t_, "v", C["_kv_v"])
                        prev = C.get(f"_qe_{gq}_{tt - 2}" if tt >= 2 else
                                     f"_qe_{gq - 1}_{NTT - 2 + tt}", 0)
                        if prev:
                            wt(t_, "v", prev)
                        for dc in range(DC):
                            mm = t_.matmul(ppj[:], xT_q[:, dc, 128 * tt:128 * (tt + 1)],
                                           wgrp[gslot][:, dc, :],
                                           start=(dc == 0), stop=(dc == DC - 1))
                        inc(mm, "t")
                        C[tmark] = C["t"]

                    @block.vector
                    def _(v_, tt=tt, ppj=ppj, gq=gq, tmark=tmark):
                        wt(v_, "t", C[tmark])
                        cv = v_.tensor_copy(prj[:], ppj[:])
                        inc(cv, "v")
                        C[f"_qev_{gq}_{tt}"] = C["v"]

                    @block.scalar
                    def _(s_, gq=gq, tt=tt):
                        wt(s_, "v", C[f"_qev_{gq}_{tt}"])
                        a1 = s_.activation(rbc[:], prj[:], AF.Square)
                        inc(a1, "a")
                        C[f"_qsa_{gq}_{tt}"] = C["a"]

                    @block.vector
                    def _(v_, gq=gq, tt=tt):
                        wt(v_, "a", C[f"_qsa_{gq}_{tt}"])
                        rv = v_.tensor_reduce(stat[:, 0:4],
                                              rbc[:].rearrange("p (h x) -> p h x", x=128),
                                              axis=AX.X, op=ALU.add)
                        inc(rv, "v")
                        C[f"_qrms_{gq}_{tt}"] = C["v"]

                    @block.scalar
                    def _(s_, gq=gq, tt=tt):
                        wt(s_, "v", C[f"_qrms_{gq}_{tt}"])
                        a1 = s_.activation(stat[:, 4:8], stat[:, 0:4], AF.Sqrt,
                                           scale=1.0 / 128.0, bias=epsrm[:, 0:1])
                        inc(a1, "a")
                        C[f"_qsq_{gq}_{tt}"] = C["a"]

                    @block.vector
                    def _(v_, tt=tt, gq=gq):
                        wt(v_, "a", C[f"_qsq_{gq}_{tt}"])
                        wt(v_, "dXB")
                        VC(v_, v_.reciprocal(stat[:, 0:4], stat[:, 4:8]))
                        VC(v_, v_.tensor_scalar(stat[:, 0:4], stat[:, 0:4], ATTN_SCALE,
                                                None, ALU.mult))
                        for hh in range(4):
                            iap = v_.tensor_scalar(prj[:, 128 * hh:128 * (hh + 1)],
                                                   prj[:, 128 * hh:128 * (hh + 1)],
                                                   stat[:, hh:hh + 1], None, ALU.mult)
                            if hh == 3:
                                VC(v_, iap)
                        x1 = prj[:].rearrange("p (h x) -> p h x", x=128)[:, :, 0:64]
                        x2 = prj[:].rearrange("p (h x) -> p h x", x=128)[:, :, 64:128]
                        o1 = rop[:].rearrange("p (h x) -> p h x", x=128)[:, :, 0:64]
                        o2 = rop[:].rearrange("p (h x) -> p h x", x=128)[:, :, 64:128]
                        tA = scr[:, 512:768].rearrange("p (h x) -> p h x", x=64)
                        tB = scr[:, 768:1024].rearrange("p (h x) -> p h x", x=64)

                        def tb(j):
                            return qtab[:, tt, j, :].unsqueeze(1).broadcast_to((128, 4, 64))
                        VC(v_, v_.tensor_tensor(tA, x1, tb(0), ALU.mult))
                        VC(v_, v_.tensor_tensor(tB, x2, tb(3), ALU.mult))
                        VC(v_, v_.tensor_tensor(o1, tA, tB, ALU.subtract))
                        VC(v_, v_.tensor_tensor(tA, x2, tb(1), ALU.mult))
                        VC(v_, v_.tensor_tensor(tB, x1, tb(2), ALU.mult))
                        cv = v_.tensor_tensor(o2, tA, tB, ALU.add)
                        inc(cv, "v")
                        C[f"_qe_{gq}_{tt}"] = C["v"]

                    @block.sync
                    def _(sy, tt=tt, gq=gq):
                        wt(sy, "v", C[f"_qe_{gq}_{tt}"])
                        d1 = ring_dma(sy, 
                            out=qT[:, 4 * gq:4 * gq + 4, 128 * tt:128 * (tt + 1)],
                            in_=rop[:], transpose=True)
                        inc(d1, "dXB", 16)
            C["_q_v"] = C["v"]
            dXB_q = C["dXB"]

            if t_qt is not None:
                @block.sync
                def _(sy):
                    wt(sy, "dXB", dXB_q)
                    inc(ring_dma(sy, out=t_qt, in_=qT[:].rearrange("p a b -> p (a b)")),
                        "dST", 16)

        if stage >= 4:
            @block.gpsimd
            def _(g_):
                g_.wait_ge(SEM["v"], C["_q_v"])
                inc(g_.iota(scr[:, 512:1024].bitcast(I32)[:, 0:512],
                            pattern=[[1, 512]], base=0, channel_multiplier=0), "p")
            p_iota2 = C["p"]

            @block.vector
            def _(v_):
                wt(v_, "v", C["_q_v"])
                wt(v_, "p", p_iota2)
                dmask = scr[:, 0:512]
                VC(v_, v_.tensor_scalar(dmask, scr[:, 512:1024].bitcast(I32)[:, 0:512],
                                        piof[:, 0:1], None, ALU.subtract))
                for k in range(4):
                    v_.tensor_scalar(masks[:, k, :], dmask, float(128 * k),
                                     None, ALU.is_lt)
                for k in range(8, 12):
                    mk = v_.tensor_scalar(masks[:, k - 4, :], dmask,
                                          float(128 * k - 1024), None, ALU.is_ge)
                VC(v_, mk)
                iv4 = v_.tensor_copy(scr[:, 512:513], piof[:])
                inc(iv4, "v")
            v_masks = C["v"]

            for g in range(2):
                for n in range(NH):
                    pe = pp[2 + (n % 2)]
                    smps = pp[4 + (n % 2)]
                    for k in range(12):
                        c = 4 * g + k
                        sp = pp[k % 2]
                        smark = f"_s_{g}_{n}_{k}"

                        @block.tensor
                        def _(t_, g=g, n=n, k=k, c=c, sp=sp, smark=smark):
                            if g == 0 and n == 0 and k == 0:
                                wt(t_, "dXB", C["dXB"])
                                wt(t_, "v", v_masks)
                            prev = C.get(f"_ex_{g}_{n}_{k - 2}" if k >= 2 else
                                         (f"_ex_{g}_{n - 1}_{10 + k}" if n >= 1 else
                                          (f"_ex_{g - 1}_{NH - 1}_{10 + k}" if g >= 1
                                           else "_none")), 0)
                            if prev:
                                wt(t_, "a", prev)
                            mm = t_.matmul(sp[:], kT[:, n // 2, 128 * c:128 * (c + 1)],
                                           qT[:, n, 512 * g:512 * (g + 1)],
                                           start=True, stop=True)
                            inc(mm, "t")
                            C[smark] = C["t"]

                        @block.scalar
                        def _(s_, g=g, n=n, k=k, c=c, sp=sp, smark=smark):
                            wt(s_, "t", C[smark])
                            if k == 0:
                                prevpv = C.get(f"_pv_{g}_{n - 1}" if n >= 1 else
                                               (f"_pv_{g - 1}_{NH - 1}" if g >= 1
                                                else "_none"), 0)
                                if prevpv:
                                    wt(s_, "t", prevpv)
                            a1 = s_.activation(ptil[:, k, :], sp[:], AF.Exp,
                                               bias=valv[:, c:c + 1], scale=1.0)
                            inc(a1, "a")
                            C[f"_ex_{g}_{n}_{k}"] = C["a"]

                    @block.vector
                    def _(v_, g=g, n=n):
                        wt(v_, "a", C[f"_ex_{g}_{n}_11"])
                        for k in (0, 1, 2, 3):
                            v_.tensor_tensor(ptil[:, k, :], ptil[:, k, :],
                                             masks[:, k, :], ALU.mult)
                        for k in (8, 9, 10, 11):
                            mv = v_.tensor_tensor(ptil[:, k, :], ptil[:, k, :],
                                                  masks[:, k - 4, :], ALU.mult)
                        inc(mv, "v")
                        C[f"_msk_{g}_{n}"] = C["v"]

                    @block.tensor
                    def _(t_, g=g, n=n, pe=pe, smps=smps):
                        wt(t_, "v", C[f"_msk_{g}_{n}"])
                        prev = C.get(f"_enc_{g}_{n - 2}" if n >= 2 else
                                     (f"_enc_{g - 1}_{NH - 2 + n}" if g >= 1
                                      else "_none"), 0)
                        if prev:
                            wt(t_, "v", prev)
                        for k in range(12):
                            c = 4 * g + k
                            t_.matmul(pe[:], v_arr[:, c, n // 2, :], ptil[:, k, :],
                                      start=(k == 0), stop=(k == 11))
                        for k in range(12):
                            mm = t_.matmul(smps[0:1, :], onesb[:], ptil[:, k, :],
                                           start=(k == 0), stop=(k == 11))
                        inc(mm, "t")
                        C[f"_pv_{g}_{n}"] = C["t"]

                    @block.vector
                    def _(v_, g=g, n=n, smps=smps):
                        wt(v_, "t", C[f"_pv_{g}_{n}"])
                        VC(v_, v_.tensor_copy(scr[0:1, 0:512], smps[0:1, :]))
                        rv = v_.reciprocal(scr[0:1, 512:1024], scr[0:1, 0:512])
                        inc(rv, "v")
                        C[f"_rc_{g}_{n}"] = C["v"]

                    @block.tensor
                    def _(t_, g=g, n=n):
                        wt(t_, "v", C[f"_rc_{g}_{n}"])
                        prev = C.get(f"_enc_{g}_{n - 1}" if n >= 1 else
                                     (f"_enc_{g - 1}_{NH - 1}" if g >= 1 else "_none"), 0)
                        if prev:
                            wt(t_, "v", prev)
                        mm = t_.matmul(pp_r[:], onesf1[:], scr[0:1, 512:1024],
                                       start=True, stop=True)
                        inc(mm, "t")
                        C[f"_bc_{g}_{n}"] = C["t"]

                    @block.vector
                    def _(v_, g=g, n=n, pe=pe):
                        wt(v_, "t", C[f"_bc_{g}_{n}"])
                        VC(v_, v_.tensor_copy(rbc[:], pp_r[:]))
                        cv = v_.tensor_tensor(enc[:, n, 512 * g:512 * (g + 1)],
                                              pe[:], rbc[:], ALU.mult)
                        inc(cv, "v")
                        C[f"_enc_{g}_{n}"] = C["v"]
            C["_attn_v"] = C["v"]
            C["_attn_t"] = C["t"]

            if t_enc is not None:
                @block.sync
                def _(sy):
                    wt(sy, "v", C["_attn_v"])
                    inc(ring_dma(sy, out=t_enc,
                                     in_=enc[:].rearrange("p a b -> p (a b)")),
                        "dST", 16)

        if stage >= 5:
            for tt in range(NTT):
                @block.sync
                def _(sy, tt=tt):
                    if tt == 0:
                        wt(sy, "t", C["_attn_t"])
                    d1 = ring_dma(sy, out=e_t[tt][:],
                                      in_=x_kv[TQ + 128 * tt:TQ + 128 * (tt + 1), :])
                    inc(d1, "dC", 16)
            dC_x = C["dC"]

            for dg in range(4):
                gslot = dg % 2
                for n in range(NH):
                    wsl = n % 2

                    @block.sync
                    def _(sy, dg=dg, n=n, wsl=wsl):
                        if dg == 0 and n == 0:
                            wt(sy, "t", C["_attn_t"])
                        if n >= 2 or dg >= 1:
                            wt(sy, "v", C[f"_oc_{dg}_{n - 2}" if n >= 2 else
                                          f"_oc_{dg - 1}_{NH - 2 + n}"])
                        d1 = ring_dma(sy, out=wstg[:, 512 * wsl:512 * (wsl + 1)],
                                          in_=okw[n, :, 512 * dg:512 * (dg + 1)])
                        inc(d1, "dW%d" % wsl, 16)

                    @block.vector
                    def _(v_, dg=dg, n=n, wsl=wsl, gslot=gslot):
                        wt(v_, "dW%d" % wsl, C["dW%d" % wsl])
                        if dg == 0 and n == 0:
                            wt(v_, "t", C["_attn_t"])
                        if dg >= 2 and n == 0:
                            wt(v_, "t", C[f"_op_{dg - 2}_7"])
                        cv = v_.tensor_copy(wo_grp[gslot][:, n, :],
                                            wstg[:, 512 * wsl:512 * (wsl + 1)])
                        inc(cv, "v")
                        C[f"_oc_{dg}_{n}"] = C["v"]
                wgready = C[f"_oc_{dg}_{NH - 1}"]
                for tt in range(NTT):
                    ppj = pp[tt % 2]
                    tmark = f"_op_{dg}_{tt}"

                    @block.tensor
                    def _(t_, dg=dg, tt=tt, ppj=ppj, gslot=gslot, wgready=wgready, tmark=tmark):
                        wt(t_, "v", wgready)
                        prev = C.get(f"_oe_{dg}_{tt - 2}" if tt >= 2 else
                                     (f"_oe_{dg - 1}_{NTT - 2 + tt}" if dg >= 1
                                      else "_none"), 0)
                        if prev:
                            wt(t_, "v", prev)
                        for n in range(NH):
                            mm = t_.matmul(ppj[:], enc[:, n, 128 * tt:128 * (tt + 1)],
                                           wo_grp[gslot][:, n, :],
                                           start=(n == 0), stop=(n == NH - 1))
                        inc(mm, "t")
                        C[tmark] = C["t"]

                    @block.vector
                    def _(v_, dg=dg, tt=tt, ppj=ppj, tmark=tmark):
                        wt(v_, "t", C[tmark])
                        if dg == 0 and tt == 0:
                            wt(v_, "dC", dC_x)
                        cv = v_.tensor_tensor(e_t[tt][:, 512 * dg:512 * (dg + 1)],
                                              e_t[tt][:, 512 * dg:512 * (dg + 1)],
                                              ppj[:], ALU.add)
                        inc(cv, "v")
                        C[f"_oe_{dg}_{tt}"] = C["v"]
            C["_op_t"] = C["t"]

            for tt in range(NTT):
                @block.scalar
                def _(s_, tt=tt):
                    s_.wait_ge(SEM["v"], C[f"_oe_3_{tt}"])
                    if tt >= 1:
                        s_.wait_ge(SEM["v"], C[f"_l1s_{tt - 1}"])
                    a1 = s_.activation(lnscr[:], e_t[tt][:], AF.Square)
                    inc(a1, "a")
                    C[f"_l1q_{tt}"] = C["a"]

                @block.vector
                def _(v_, tt=tt):
                    wt(v_, "v", C[f"_oe_3_{tt}"])
                    wt(v_, "a", C[f"_l1q_{tt}"])
                    et = e_t[tt][:]
                    VC(v_, v_.tensor_reduce(stat[:, 0:1], et, axis=AX.X, op=ALU.add))
                    VC(v_, v_.tensor_reduce(stat[:, 1:2], lnscr[:], axis=AX.X, op=ALU.add))
                    VC(v_, v_.tensor_scalar(stat[:, 0:1], stat[:, 0:1], 1.0 / D, None, ALU.mult))
                    VC(v_, v_.tensor_scalar(stat[:, 1:2], stat[:, 1:2], 1.0 / D, None, ALU.mult))
                    VC(v_, v_.tensor_scalar(stat[:, 2:3], stat[:, 0:1], stat[:, 0:1],
                                            None, ALU.mult))
                    cv = v_.tensor_tensor(stat[:, 1:2], stat[:, 1:2], stat[:, 2:3],
                                          ALU.subtract)
                    inc(cv, "v")
                    C[f"_l1s_{tt}"] = C["v"]

                @block.scalar
                def _(s_, tt=tt):
                    wt(s_, "v", C[f"_l1s_{tt}"])
                    aln = s_.activation(stat[:, 2:3], stat[:, 1:2], AF.Ln, bias=epsln[:, 0:1])
                    inc(aln, "a")
                    wt(s_, "a")
                    a1 = s_.activation(stat[:, 3:4], stat[:, 2:3], AF.Exp, scale=-0.5)
                    inc(a1, "a")
                    C[f"_l1r_{tt}"] = C["a"]

                @block.vector
                def _(v_, tt=tt):
                    wt(v_, "a", C[f"_l1r_{tt}"])
                    et = e_t[tt][:]
                    v_.tensor_scalar(et, et, stat[:, 0:1], stat[:, 3:4],
                                     ALU.subtract, ALU.mult)
                    ev = et.rearrange("p (a two) -> p two a", two=2)
                    if tt >= 1:
                        wt(v_, "dXB", 16 * 0 + C["dXB"])
                    v_.tensor_copy(yev[:, 0:1024], ev[:, 0, :])
                    cv = v_.tensor_copy(yev[:, 1024:2048], ev[:, 1, :])
                    inc(cv, "v")
                    C[f"_ysp_{tt}"] = C["v"]

                @block.sync
                def _(sy, tt=tt):
                    wt(sy, "v", C[f"_ysp_{tt}"])
                    if tt == 0:
                        wt(sy, "t", C["_op_t"])
                    d0 = ring_dma(sy, out=yTe[:, :, 128 * tt:128 * (tt + 1)],
                                      in_=yev[:, 0:1024], transpose=True)
                    inc(d0, "dXB", 16)
                    d1 = ring_dma(sy, out=yTo[:, :, 128 * tt:128 * (tt + 1)],
                                      in_=yev[:, 1024:2048], transpose=True)
                    inc(d1, "dXB", 16)
            C["_y_xb"] = C["dXB"]
            C["_ln1_v"] = C["v"]

            if t_y is not None:
                for tt in range(NTT):
                    @block.sync
                    def _(sy, tt=tt):
                        wt(sy, "v", C["_ln1_v"])
                        inc(ring_dma(sy, out=t_y[128 * tt:128 * (tt + 1), :],
                                         in_=e_t[tt][:]), "dST", 16)

        if stage >= 6:
            LUT_SPEC = [
                (7, 6, 12),
                (0x70, 2, 8),
            ]

            def gu_dequant(v_, s_):
                sl4 = s_ % 4
                psl = s_ % 2
                wt(v_, "dPK%d" % (s_ % 2), 64 * (s_ // 2 + 1))
                if s_ >= 2:
                    wt(v_, "dT%d" % psl, 16 * (s_ // 2))
                w = gupk[sl4][:].rearrange("p a b -> p (a b)")
                T, S = dscT, dscS
                fv = T.bitcast(BF16)
                fv3 = fv.rearrange("p (m w) -> p m w", m=2)
                S3 = S.rearrange("p (m w) -> p m w", m=2)
                for ni, (msk, shl, sshl) in enumerate(LUT_SPEC):
                    gb_bf = plv_bf[psl][:, :, ni, :]
                    gb_u = plv_u16[psl][:, :, ni, :]
                    v_.tensor_scalar(T, w, msk, shl, ALU.bitwise_and,
                                     ALU.logical_shift_left)
                    v_.tensor_scalar(T, T, 16128, None, ALU.add)
                    v_.tensor_scalar(S, w, sshl, 0x8000, ALU.logical_shift_left,
                                     ALU.bitwise_and)
                    v_.tensor_scalar(gb_bf, fv3, 1.0, 0.0, ALU.subtract, ALU.min)
                    v_.tensor_tensor(gb_bf, gb_bf, fv3, ALU.add)
                    v_.tensor_tensor(gb_u, gb_u, S3, ALU.bitwise_xor)
                    for m in range(2):
                        sca = gscf[:, 2 * sl4 + m, :].unsqueeze(2).broadcast_to(
                            (128, 64, 16))
                        pm = plv_bf[psl][:, m, ni, :].rearrange(
                            "p (b r) -> p b r", r=16)
                        last = v_.tensor_tensor(pm, pm, sca, ALU.mult)
                inc(last, "v")
                C[f"_dq_{s_}"] = C["v"]

            def dn_dequant(v_, k):
                dsl = 2 if (k % GSETS) == 5 else (k % 2)
                wt(v_, "dDN%d" % (k % 2), 64 * (k // 2 + 1))
                if k >= 1:
                    wt(v_, "dU%d" % (((k - 1) // 2) % 2))
                w = dnpk[dsl][:].rearrange("p a b -> p (a b)")
                T, S = dscT[:, 0:1024], dscS[:, 0:1024]
                fv = T.bitcast(BF16)
                fv3 = fv.rearrange("p (m w) -> p m w", m=2)
                S3 = S.rearrange("p (m w) -> p m w", m=2)
                for ni, (msk, shl, sshl) in enumerate(LUT_SPEC):
                    gb_bf = dpl_bf[:, :, ni, :]
                    gb_u = dpl_u16[:, :, ni, :]
                    v_.tensor_scalar(T, w, msk, shl, ALU.bitwise_and,
                                     ALU.logical_shift_left)
                    v_.tensor_scalar(T, T, 16128, None, ALU.add)
                    v_.tensor_scalar(S, w, sshl, 0x8000, ALU.logical_shift_left,
                                     ALU.bitwise_and)
                    v_.tensor_scalar(gb_bf, fv3, 1.0, 0.0, ALU.subtract, ALU.min)
                    v_.tensor_tensor(gb_bf, gb_bf, fv3, ALU.add)
                    v_.tensor_tensor(gb_u, gb_u, S3, ALU.bitwise_xor)
                    for i2 in range(2):
                        sca = dscf[:, 2 * dsl + i2, :].unsqueeze(2).broadcast_to(
                            (128, 32, 16))
                        pm = dpl_bf[:, i2, ni, :].rearrange("p (b r) -> p b r", r=16)
                        last = v_.tensor_tensor(pm, pm, sca, ALU.mult)
                inc(last, "v")
                C[f"_dndq_{k}"] = C["v"]

            g_pk2 = g_pk.rearrange("(r two) w -> two r w", two=2)
            u_pk2 = u_pk.rearrange("(r two) w -> two r w", two=2)
            g_sc2 = g_sc.rearrange("(r two) w -> two r w", two=2)
            u_sc2 = u_sc.rearrange("(r two) w -> two r w", two=2)
            glsl = [glscr,
        qtab[:].rearrange("p a b c -> p (a b c)")[:, 0:256].bitcast(BF16)]

            def gu_loads(sy, t):
                slt = t % 4
                part, jt = (t % GSETS) % 2, (t % GSETS) // 2
                r0h = 512 * (t // GSETS) + 128 * jt
                sem = "dPK%d" % (t % 2)
                ring_dma(sy, out=gupk[slt][:, 0, :],
                         in_=g_pk2[part, r0h:r0h + 128, :]).then_inc(SEM[sem], 16)
                ring_dma(sy, out=gupk[slt][:, 1, :],
                         in_=u_pk2[part, r0h:r0h + 128, :]).then_inc(SEM[sem], 16)
                ring_dma(sy, out=gscf[:, 2 * slt + 0, :],
                         in_=g_sc2[part, r0h:r0h + 128, :]).then_inc(SEM[sem], 16)
                d1 = ring_dma(sy, out=gscf[:, 2 * slt + 1, :],
                              in_=u_sc2[part, r0h:r0h + 128, :])
                inc(d1, sem, 16)
                C[sem] += 48

            def dn_loads(sy, k):
                dsl = 2 if (k % GSETS) == 5 else (k % 2)
                sem = "dDN%d" % (k % 2)
                gi_ = k // GSETS
                hg_ = (k % GSETS) // 2
                hc0 = 4 * hg_ + 2 * (k % 2)
                for i2 in range(2):
                    hc = hc0 + i2
                    ring_dma(sy, out=dnpk[dsl][:, i2, :],
                             in_=d_pk[128 * hc:128 * (hc + 1),
                                      512 * gi_:512 * (gi_ + 1)]).then_inc(SEM[sem], 16)
                    d1 = ring_dma(sy, out=dscf[:, 2 * dsl + i2, :],
                                  in_=d_sc[128 * hc:128 * (hc + 1),
                                           32 * gi_:32 * (gi_ + 1)])
                    if i2 == 1:
                        inc(d1, sem, 16)
                        C[sem] += 48
                    else:
                        d1.then_inc(SEM[sem], 16)

            def dn_xbar_section(k):
                @block.sync
                def _(sy, k=k):
                    wt(sy, "v", C[f"_dndq_{k}"])
                    hgg = k // 2
                    ws2 = hgg % 2
                    sbr = k % 2
                    if hgg >= 2 and sbr == 0:
                        wt(sy, "t", C.get(f"_dmm_{hgg - 2}", 0))
                    for i2 in range(2):
                        hcL = 2 * sbr + i2
                        d1 = ring_dma(sy,
                                      out=WdT[ws2][:, :, 128 * hcL:128 * (hcL + 1)],
                                      in_=dplanes[:, i2, :], transpose=True)
                        if i2 == 1:
                            inc(d1, "dU%d" % ws2, 16)
                            C["dU%d" % ws2] += 16
                        else:
                            d1.then_inc(SEM["dU%d" % ws2], 16)

            def emit_evac(s_, th):
                u = 2 * s_ + th
                pu = pp[2 * (u % 3) + 1]

                @block.vector
                def _(v_, th=th, u=u, slo=s_ % GSETS, pu=pu, gi=s_ // GSETS):
                    wt(v_, "a", C[f"_gl_{u}"])
                    if gi >= 1 and u % (2 * GSETS) < 2:
                        wt(v_, "t", C.get(f"_dn_t_{gi - 1}", 0))
                    cv = v_.tensor_tensor(
                        inter[:, slo, 512 * th:512 * (th + 1)],
                        glsl[u % 2], pu[:], ALU.mult)
                    inc(cv, "v")
                    C[f"_ie_{u}"] = C["v"]

            _dg = DOWN_GRPS if DOWN_GRPS is not None else list(range(NGRP))

            def down_block(gi):
                if gi not in _dg:
                    C[f"_dn_t_{gi}"] = C["t"]
                    for tt in range(NTT):
                        C[f"_de_{gi}_3_{tt}"] = C["v"]
                    return

                if t_int is not None and gi == 0:
                    @block.sync
                    def _(sy):
                        wt(sy, "v", C[f"_ie_{2 * GSETS - 1}"])
                        inc(ring_dma(sy, out=t_int,
                                         in_=inter[:].rearrange("p a b -> p (a b)")),
                            "dST", 16)

                for hg in range(4):
                    hgg = 4 * gi + hg
                    ws2 = hgg % 2
                    for tt in range(NTT):
                        @block.tensor
                        def _(t_, gi=gi, hg=hg, hgg=hgg, ws2=ws2, tt=tt):
                            if tt == 0:
                                wt(t_, "dU%d" % ws2, 64 * (hgg // 2 + 1))
                                wt(t_, "v", C[f"_ie_{2 * GSETS * (gi + 1) - 1}"])
                            if DBG_CB is not None and gi == 0 and hg == 0 and tt == 0:
                                t_.dbg_aps(DBG_CB, inter[:, :, 0:128],
                                           WdT[ws2][:])
                            pd = (pp_r, pp6)[tt % 2]
                            prev = C.get(f"_de_{gi}_{hg}_{tt - 2}" if tt >= 2 else
                                         (f"_de_{gi}_{hg - 1}_{NTT - 2 + tt}" if hg >= 1 else
                                          (f"_de_{gi - 1}_3_{NTT - 2 + tt}" if gi >= 1
                                           else "_none")), 0)
                            if prev:
                                t_.wait_ge(SEM["v"], prev)
                            for slo in range(GSETS):
                                par, j = slo % 2, slo // 2
                                mm = t_.matmul(pd[:], inter[:, slo, 128 * tt:128 * (tt + 1)],
                                               WdT[ws2][:, 4 * par + j, :],
                                               start=(slo == 0), stop=(slo == GSETS - 1))
                            inc(mm, "t")
                            C[f"_dmmt_{hgg}_{tt}"] = C["t"]
                            if DBG_CB is not None and gi == 0 and hg == 0 and tt == 0:
                                t_.dbg_aps(DBG_CB, pd[:])

                        @block.vector
                        def _(v_, gi=gi, hg=hg, hgg=hgg, tt=tt):
                            pd = (pp_r, pp6)[tt % 2]
                            wt(v_, "t", C[f"_dmmt_{hgg}_{tt}"])
                            if DBG_CB is not None and gi == 0 and hg == 0 and tt == 0:
                                v_.dbg_aps(DBG_CB, pd[:], e_t[tt][:, 0:128])
                            cv = v_.tensor_tensor(e_t[tt][:, 512 * hg:512 * (hg + 1)],
                                                  e_t[tt][:, 512 * hg:512 * (hg + 1)],
                                                  pd[:], ALU.add)
                            inc(cv, "v")
                            C[f"_de_{gi}_{hg}_{tt}"] = C["v"]
                    C[f"_dmm_{hgg}"] = C["t"]
                    if hg < 2:
                        k1 = GSETS * gi + 4 + 2 * hg
                        dn_xbar_section(k1)
                        for kq in (k1 + 1, k1 + 2):
                            if kq < GSETS * gi + GSETS:
                                @block.vector
                                def _(v_, k=kq):
                                    dn_dequant(v_, k)
                                if kq == k1 + 1:
                                    dn_xbar_section(kq)
                C[f"_dn_t_{gi}"] = C["t"]

            for s_ in range(GSETS * NGRP):
                gi, slo = divmod(s_, GSETS)
                psl = s_ % 2
                ws = s_ % 2
                kd = s_ - 2
                kd_on = kd >= 0 and (kd // GSETS) in _dg

                if s_ == 0:
                    @block.sync
                    def _(sy):
                        wt(sy, "v", C["_ln1_v"])
                        wt(sy, "dXB", C["_y_xb"])
                        for t in range(4):
                            gu_loads(sy, t)

                @block.vector
                def _(v_, s_=s_):
                    gu_dequant(v_, s_)

                if kd_on:
                    @block.sync
                    def _(sy, s_=s_, kd=kd):
                        wt(sy, "v", C[f"_dq_{s_ - 1}"])
                        dn_loads(sy, kd)

                if slo == 1 and gi >= 1:
                    down_block(gi - 1)

                if s_ >= 1:
                    emit_evac(s_ - 1, 0)
                    emit_evac(s_ - 1, 1)

                @block.sync
                def _(sy, s_=s_, psl=psl, ws=ws, kd=kd, kd_on=kd_on):
                    wt(sy, "v", C[f"_dq_{s_}"])
                    if s_ >= 2:
                        wt(sy, "t", C[f"_gu_{s_ - 2}_1"])
                    d1 = ring_dma(sy, out=WT[ws][:],
                                      in_=planes[psl][:].rearrange("p a b -> p (a b)"),
                                      transpose=True)
                    inc(d1, "dT%d" % ws, 16)
                    if s_ + 4 < GSETS * NGRP:
                        gu_loads(sy, s_ + 4)

                if kd_on and kd % GSETS <= 4:
                    @block.vector
                    def _(v_, k=kd):
                        dn_dequant(v_, k)

                if kd_on and kd % GSETS <= 3:
                    dn_xbar_section(kd)

                for th in range(2):
                        u = 2 * s_ + th
                        pg = pp[2 * (u % 3)]
                        pu = pp[2 * (u % 3) + 1]

                        @block.tensor
                        def _(t_, s_=s_, th=th, u=u, ws=ws, pg=pg, pu=pu, gi=gi):
                            if th == 0:
                                wt(t_, "dT%d" % ws, 16 * (s_ // 2 + 1))
                                if s_ == 0:
                                    wt(t_, "dXB", C["_y_xb"])
                            prev = C.get(f"_ie_{u - 3}", 0)
                            if prev:
                                wt(t_, "v", prev)
                            for jj in range(8):
                                t_.matmul(pg[:], WT[ws][:, jj, :],
                                          yTe[:, jj, 512 * th:512 * (th + 1)],
                                          start=(jj == 0), stop=False)
                            for jj in range(8):
                                t_.matmul(pg[:], WT[ws][:, 8 + jj, :],
                                          yTo[:, jj, 512 * th:512 * (th + 1)],
                                          start=False, stop=(jj == 7))
                            for jj in range(8):
                                t_.matmul(pu[:], WT[ws][:, 16 + jj, :],
                                          yTe[:, jj, 512 * th:512 * (th + 1)],
                                          start=(jj == 0), stop=False)
                            for jj in range(8):
                                mm = t_.matmul(pu[:], WT[ws][:, 24 + jj, :],
                                               yTo[:, jj, 512 * th:512 * (th + 1)],
                                               start=False, stop=(jj == 7))
                            inc(mm, "t")
                            C[f"_gu_{s_}_{th}"] = C["t"]

                        @block.scalar
                        def _(s2, s_=s_, th=th, u=u, pg=pg):
                            wt(s2, "t", C[f"_gu_{s_}_{th}"])
                            prev = C.get(f"_ie_{u - 2}", 0)
                            if prev:
                                wt(s2, "v", prev)
                            a1 = s2.activation(glsl[u % 2], pg[:], AF.Gelu_apprx_tanh)
                            inc(a1, "a")
                            C[f"_gl_{u}"] = C["a"]

            NS = GSETS * NGRP
            if (NGRP - 1) in _dg:
                @block.sync
                def _(sy):
                    dn_loads(sy, NS - 2)
                    dn_loads(sy, NS - 1)
            emit_evac(NS - 1, 0)
            emit_evac(NS - 1, 1)
            down_block(NGRP - 1)

        if t_e2 is not None and stage >= 6:
            for tt in range(NTT):
                @block.sync
                def _(sy, tt=tt):
                    wt(sy, "v", C[f"_de_{NGRP - 1}_3_{tt}"])
                    inc(ring_dma(sy, out=t_e2[128 * tt:128 * (tt + 1), :],
                                 in_=e_t[tt][:]), "dST", 16)

        if stage >= 7:
            for tt in range(NTT):
                @block.scalar
                def _(s_, tt=tt):
                    s_.wait_ge(SEM["v"], C[f"_de_{NGRP - 1}_3_{tt}"])
                    if tt >= 1:
                        s_.wait_ge(SEM["v"], C[f"_l2s_{tt - 1}"])
                    a1 = s_.activation(lnscr[:], e_t[tt][:], AF.Square)
                    inc(a1, "a")
                    C[f"_l2q_{tt}"] = C["a"]

                @block.vector
                def _(v_, tt=tt):
                    wt(v_, "v", C[f"_de_{NGRP - 1}_3_{tt}"])
                    wt(v_, "a", C[f"_l2q_{tt}"])
                    et = e_t[tt][:]
                    VC(v_, v_.tensor_reduce(stat[:, 0:1], et, axis=AX.X, op=ALU.add))
                    VC(v_, v_.tensor_reduce(stat[:, 1:2], lnscr[:], axis=AX.X, op=ALU.add))
                    VC(v_, v_.tensor_scalar(stat[:, 0:1], stat[:, 0:1], 1.0 / D, None, ALU.mult))
                    VC(v_, v_.tensor_scalar(stat[:, 1:2], stat[:, 1:2], 1.0 / D, None, ALU.mult))
                    VC(v_, v_.tensor_scalar(stat[:, 2:3], stat[:, 0:1], stat[:, 0:1],
                                            None, ALU.mult))
                    cv = v_.tensor_tensor(stat[:, 1:2], stat[:, 1:2], stat[:, 2:3],
                                          ALU.subtract)
                    inc(cv, "v")
                    C[f"_l2s_{tt}"] = C["v"]

                @block.scalar
                def _(s_, tt=tt):
                    wt(s_, "v", C[f"_l2s_{tt}"])
                    aln = s_.activation(stat[:, 2:3], stat[:, 1:2], AF.Ln, bias=epsln[:, 0:1])
                    inc(aln, "a")
                    wt(s_, "a")
                    a1 = s_.activation(stat[:, 3:4], stat[:, 2:3], AF.Exp, scale=-0.5)
                    inc(a1, "a")
                    C[f"_l2r_{tt}"] = C["a"]

                @block.vector
                def _(v_, tt=tt):
                    wt(v_, "a", C[f"_l2r_{tt}"])
                    cv = v_.tensor_scalar(e_t[tt][:], e_t[tt][:], stat[:, 0:1],
                                          stat[:, 3:4], ALU.subtract, ALU.mult)
                    inc(cv, "v")
                    C[f"_l2a_{tt}"] = C["v"]

                @block.sync
                def _(sy, tt=tt):
                    wt(sy, "v", C[f"_l2a_{tt}"])
                    d1 = ring_dma(sy, out=out[128 * tt:128 * (tt + 1), :],
                                      in_=e_t[tt][:])
                    inc(d1, "dST", 16)
        else:
            @block.sync
            def _(sy):
                for tt in range(NTT):
                    d1 = ring_dma(sy, out=out[128 * tt:128 * (tt + 1), :],
                                      in_=e_t[tt][:])
                    inc(d1, "dST", 16)

        @block.sync
        def _(sy):
            wt(sy, "dST")

    return nc, tap_aps


def _shards(inputs):
    x = np.asarray(inputs["L0_attn_x"])
    qkw = np.ascontiguousarray(np.asarray(inputs["L0_attn_q_kernel"]))
    kvw = np.ascontiguousarray(np.asarray(inputs["L0_attn_kv_kernel"]))
    okw = np.ascontiguousarray(np.asarray(inputs["L0_attn_out_kernel"]))
    qnsv = np.ascontiguousarray(np.asarray(inputs["L0_attn_q_norm_scale"])).reshape(1, HD)
    knsv = np.ascontiguousarray(np.asarray(inputs["L0_attn_k_norm_scale"])).reshape(1, HD)
    gpk = np.asarray(inputs["L0_ffn_gate_W_packed"]).astype(np.uint16)
    gsc = np.ascontiguousarray(np.asarray(inputs["L0_ffn_gate_scales"]))
    upk = np.asarray(inputs["L0_ffn_up_W_packed"]).astype(np.uint16)
    usc = np.ascontiguousarray(np.asarray(inputs["L0_ffn_up_scales"]))
    dpk = np.asarray(inputs["L0_ffn_down_W_packed"]).astype(np.uint16)
    dsc = np.ascontiguousarray(np.asarray(inputs["L0_ffn_down_scales"]))
    invts = np.ascontiguousarray(np.broadcast_to(
        (1.0 / (10000.0 ** (np.arange(64, dtype=np.float64) / 64.0))
         ).astype(np.float32)[None, :], (128, 64)))
    maps = []
    for core in range(8):
        b, h = core // 2, core % 2
        if h == 0:
            xkv = np.concatenate([np.zeros((1024, D), np.float32), x[b, 0:1024]],
                                 axis=0)
            val = np.concatenate([np.full(1024, -1e30, np.float32),
                                  np.zeros(1024, np.float32)])
            kof = np.full(128, -1024.0, np.float32)
        else:
            xkv = np.ascontiguousarray(x[b])
            val = np.zeros(2048, np.float32)
            kof = np.zeros(128, np.float32)
        maps.append({
            "x_kv": xkv, "val_in": val, "koff_in": kof, "invts_in": invts,
            "qkw": qkw, "kvk": kvw, "okw": okw, "qns": qnsv, "kns": knsv,
            "g_pk": gpk, "g_sc": gsc, "u_pk": upk, "u_sc": usc,
            "d_pk": dpk, "d_sc": dsc,
        })
    return maps


def kernel(**inputs):
    maps = _shards(inputs)
    nc, _ = build_program(taps=TAPS, stage=STAGE)
    res = run_bass_kernel_spmd(nc, maps, core_ids=list(range(8)))
    outp = np.empty((4, 2048, 2048), np.float32)
    for core in range(8):
        b, h = core // 2, core % 2
        outp[b, 1024 * h:1024 * (h + 1), :] = res.results[core]["out"]
    return outp
